# version 22
# baseline (speedup 1.0000x reference)
import numpy as np
import concourse.bass as bass
import concourse.mybir as mybir
from concourse.bass_utils import run_bass_kernel_spmd

F32 = mybir.dt.float32
BF16 = mybir.dt.bfloat16
AF = mybir.ActivationFunctionType
ALU = mybir.AluOpType

B, S, D = 4, 8192, 1024
H, DH, WIN_R = 16, 64, 256
DFF = 4096
EPS = 1e-5
NCORES = 8
TCORE = S // 2
HALO = 256
TC = 512
NCHUNK = TCORE // TC
WINT = TC + 2 * HALO
NTT = WINT // 128
NQT = TC // 128
NFB = DFF // 128
NDT = D // 128

DEBUG_TAPS = []
P1_STAGE = 99

ENGS = ("pe", "dve", "sc", "sp", "gp")


class _Buf:
    __slots__ = ("name", "writer", "readers")

    def __init__(self, name):
        self.name = name
        self.writer = None
        self.readers = []


class _Op:
    __slots__ = ("i", "eng", "fn", "dma", "deps", "inc", "marker", "tag",
                 "q", "barrier", "drain")

    def __init__(self, i, eng, fn, dma, tag, q=None, barrier=False):
        self.i, self.eng, self.fn, self.dma, self.tag = i, eng, fn, dma, tag
        self.q, self.barrier = q, barrier
        self.deps = []
        self.inc = False
        self.marker = None
        self.drain = False


class _Sched:

    def __init__(self):
        self.ops = []

    def op(self, eng, fn, reads=(), writes=(), dma=False, tag="", q=None,
           barrier=False):
        if dma:
            assert q is not None, f"dma op {tag} needs a qsem key"
        o = _Op(len(self.ops), eng, fn, dma, tag, q=q, barrier=barrier)
        deps = set()
        def _edge(w):
            if w.eng != eng or w.dma or dma or eng == "gp":
                deps.add(w.i)
            elif eng != "pe":
                o.drain = True
        for b in reads:
            assert b.writer is not None, f"read of uninitialized buf {b.name} by {tag}"
            _edge(b.writer)
        for b in writes:
            w = b.writer
            if w is not None:
                _edge(w)
            for r in b.readers:
                if r.i in deps:
                    continue
                _edge(r)
        o.deps = sorted(deps)
        for b in reads:
            b.readers.append(o)
        for b in writes:
            b.writer = o
            b.readers = []
        self.ops.append(o)
        return o

    def emit(self, nc, sems, qsems):
        for o in self.ops:
            for d in o.deps:
                dep = self.ops[d]
                if not dep.dma:
                    dep.inc = True
        inc_cnt = {e: 0 for e in ENGS}
        q_cnt = {}
        q_tot = {}
        for o in self.ops:
            if o.dma:
                q_tot[o.q] = q_tot.get(o.q, 0) + 16
        for o in self.ops:
            if o.dma:
                q_cnt[o.q] = q_cnt.get(o.q, 0) + 16
                val = q_tot[o.q] if o.barrier else q_cnt[o.q]
                o.marker = (qsems[o.q], val)
            elif o.inc:
                inc_cnt[o.eng] += 1
                o.marker = (sems[o.eng], inc_cnt[o.eng])
        self.final_q = dict(q_tot)
        self.final_inc = dict(inc_cnt)
        streams = {e: [o for o in self.ops if o.eng == e] for e in ENGS}

        def run_stream(eng_name, e):
            hw = {}
            dirty = [False]
            for o in streams[eng_name]:
                for d in o.deps:
                    m = self.ops[d].marker
                    assert m is not None, (
                        f"op {o.i}({o.tag}) waits on {d}"
                        f"({self.ops[d].tag}) which has no marker")
                    sem, val = m
                    key = id(sem)
                    if hw.get(key, 0) < val:
                        e.wait_ge(sem, val)
                        hw[key] = val
                if o.drain and dirty[0]:
                    e.drain()
                    dirty[0] = False
                inst = o.fn(e)
                dirty[0] = True
                if o.dma:
                    inst.then_inc(qsems[o.q], 16)
                elif o.inc:
                    inst.then_inc(sems[o.eng], 1)
        return run_stream

    def qkeys(self):
        return sorted({o.q for o in self.ops if o.dma}, key=str)


def _ceil(a, b):
    return -(-a // b)


def build_nc():
    nc = bass.Bass("TRN2", target_bir_lowering=False)

    xh = nc.dram_tensor("xh", [TCORE + 2 * HALO, D], F32, kind="ExternalInput").ap()
    w1 = nc.dram_tensor("w1", [D, DFF], F32, kind="ExternalInput").ap()
    w2 = nc.dram_tensor("w2", [DFF, D], F32, kind="ExternalInput").ap()
    wfc = nc.dram_tensor("wfc", [DFF, D], F32, kind="ExternalInput").ap()
    wpr = nc.dram_tensor("wpr", [D, DFF], F32, kind="ExternalInput").ap()
    g1 = nc.dram_tensor("g1", [D], F32, kind="ExternalInput").ap()
    b1ln = nc.dram_tensor("b1ln", [D], F32, kind="ExternalInput").ap()
    g2 = nc.dram_tensor("g2", [D], F32, kind="ExternalInput").ap()
    b2ln = nc.dram_tensor("b2ln", [D], F32, kind="ExternalInput").ap()
    bias1 = nc.dram_tensor("bias1", [DFF], F32, kind="ExternalInput").ap()
    bias2 = nc.dram_tensor("bias2", [D], F32, kind="ExternalInput").ap()
    fastb = nc.dram_tensor("fastb", [16, 16], F32, kind="ExternalInput").ap()
    uu = nc.dram_tensor("uu", [DFF, 16], F32, kind="ExternalInput").ap()
    vv = nc.dram_tensor("vv", [16, D], F32, kind="ExternalInput").ap()
    gate = nc.dram_tensor("gate", [1, 1], F32, kind="ExternalInput").ap()
    y = nc.dram_tensor("y", [TCORE, D], F32, kind="ExternalOutput").ap()

    st_w1 = nc.dram_tensor("st_w1", [NFB, 128, D], BF16, kind="Internal").ap()
    st_w2 = nc.dram_tensor("st_w2", [NFB, 128, D], BF16, kind="Internal").ap()
    st_wfcT = nc.dram_tensor("st_wfcT", [NFB, 128, D], BF16, kind="Internal").ap()
    st_wprT = nc.dram_tensor("st_wprT", [NFB, 128, D], BF16, kind="Internal").ap()

    dbg = {}
    for name in DEBUG_TAPS:
        shape = {
            "vp": [NCHUNK, NTT, 128, H, 65],
            "xnT": [NCHUNK, NDT, 128, WINT],
            "m": [NCHUNK, 16, TC],
            "ln1in": [NCHUNK, NQT, 128, D],
            "x1T": [NCHUNK, NDT, 128, TC],
            "x1tok": [NCHUNK, NQT, 128, D],
            "ln2in": [NCHUNK, NQT, 128, D],
            "rfb": [NCHUNK, NFB, 128, TC],
            "sfc": [NCHUNK, NFB, 128, TC],
            "g2z2": [NCHUNK, NQT, 128, D],
            "stw": [4, NFB, 128, D],
        }[name]
        dt = BF16 if name in ("vp", "xnT", "x1T", "x1tok", "rfb", "sfc", "g2z2",
                              "stw", "m") else F32
        dbg[name] = nc.dram_tensor("dbg_" + name, shape, dt, kind="Internal" if False else "ExternalOutput").ap()

    sch = _Sched()

    import contextlib
    stack = contextlib.ExitStack()
    sb = lambda name, shape, dtype: stack.enter_context(
        nc.sbuf_tensor(name, shape, dtype))

    ident_bf = sb("ident_bf", [128, 128], BF16)
    constcol = sb("constcol", [128, 8], F32)
    ident_f32 = sb("ident_f32", [128, 128], F32)
    maskband = sb("maskband", [128, 640], BF16)
    headmask = sb("headmask", [128, NDT, 16], BF16)
    ones1p = sb("ones1p", [1, 128], BF16)
    g1b1_sb = sb("g1b1_sb", [128, 2 * NDT], F32)
    bias1_sb = sb("bias1_sb", [128, NFB], F32)
    b2row = sb("b2row", [1, D], F32)
    b1rowT = sb("b1rowT", [1, D], F32)
    bt2row = sb("bt2row", [1, D], F32)
    ones1pf = sb("ones1pf", [1, 128], F32)
    fbT_f32 = sb("fbT_f32", [16, 16], F32)
    gate_f = sb("gate_f", [1, 1], F32)

    g1bc = sb("g1bc", [128, D], BF16)
    g2bc = sb("g2bc", [128, D], BF16)
    vT_sb = sb("vT_sb", [128, NDT * 16], BF16)
    fbT_sb = sb("fbT_sb", [16, 16], BF16)
    uTg_sb = sb("uTg_sb", [16, DFF], BF16)
    gate16 = sb("gate16", [16, 1], F32)

    wring = sb("wring", [128, 4, D], BF16)
    NRING = 4

    prep_f32 = sb("prep_f32", [128, DFF], F32)
    g1row = prep_f32[0:1, 0:D]
    g2row = prep_f32[0:1, D:2 * D]
    prep_a = sb("prep_a", [128, DFF], BF16)
    prep_st = sb("prep_st", [128, D], BF16)
    prep_vec = sb("prep_vec", [128, 8], F32)
    prscale = sb("prscale", [128, 4 * NDT], F32)

    xbuf = sb("xbuf", [128, 2, D], F32)
    outbuf = sb("outbuf", [128, 2, D], F32)

    vp = sb("vp", [128, NTT, H, 65], BF16)
    xnT = sb("xnT", [128, NDT, WINT], BF16)
    xnT2 = sb("xnT2", [128, 2, 128], BF16)
    m16 = sb("m16", [16, TC], BF16)
    qaug = sb("qaug", [65, 2, TC], BF16)
    kaug = sb("kaug", [65, 2, WINT], BF16)
    psb = sb("psb", [128, 3, 640], BF16)
    avT = sb("avT", [65, 2, TC], F32)
    rz = sb("rz", [128, NQT], F32)
    ln1in = sb("ln1in", [128, NQT, D], F32)
    t1s = sb("t1s", [128, NQT, D], BF16)
    x1T = sb("x1T", [128, NDT, TC], BF16)
    x1tok = sb("x1tok", [128, NQT, D], BF16)
    pool = sb("pool", [128, NFB, TC], BF16)
    lr_sb = sb("lr_sb", [16, 2, TC], BF16)
    stat = sb("stat", [128, 64], F32)
    sqscr = sb("sqscr", [128, D], BF16)

    psum = stack.enter_context(nc.psum_tensor("pp", [128, 4096], F32))

    bufs = {}

    def B_(*key):
        if key not in bufs:
            bufs[key] = _Buf(str(key))
        return bufs[key]

    def PB(i):
        return B_("psum", i)

    psum_bf = psum.bitcast(BF16)

    def bank(i, lo=0, hi=512, part=128):
        return psum[0:part, 512 * i + lo: 512 * i + hi]

    def bank_bf(i, lo=0, hi=1024, part=128):
        return psum_bf[0:part, 1024 * i + lo: 1024 * i + hi]

    def PQ(i):
        return B_("pf32q", i)

    def E8(i, w=512):
        return prep_f32[:, 512 * i:512 * i + w]

    def EB(i):
        return B_("pf32e", i)

    def dma(eng, out_ap, in_ap, reads, writes, tag, slow=False, q=None,
            barrier=False):
        return sch.op(eng, lambda e, o=out_ap, i=in_ap, s=slow: e.dma_start(
            out=o, in_=i, allow_slow_non_contiguous=s),
            reads=reads, writes=writes, dma=True, tag=tag, q=q, barrier=barrier)

    for nm, ap_ in (("ident_bf", ident_bf), ("ident_f32", ident_f32)):
        sch.op("gp", lambda e, ap_=ap_: e.memset(ap_[:, :], 0.0), [],
               [B_(nm)], tag=nm + "_z")
        sch.op("gp", lambda e, ap_=ap_: e.affine_select(
            out=ap_[:, :], in_=ap_[:, :], compare_op=ALU.not_equal,
            fill=1.0, base=0, pattern=[[-1, 128]], channel_multiplier=1),
            [B_(nm)], [B_(nm)], tag=nm)

    sch.op("gp", lambda e: e.memset(maskband[:, :], 1.0), [], [B_("maskband")],
           tag="mb_z")
    sch.op("gp", lambda e: e.affine_select(
        out=maskband[:, :], in_=maskband[:, :], compare_op=ALU.is_ge,
        fill=0.0, base=0, pattern=[[1, 640]], channel_multiplier=-1),
        [B_("maskband")], [B_("maskband")], tag="mb_1")
    sch.op("gp", lambda e: e.affine_select(
        out=maskband[:, :], in_=maskband[:, :], compare_op=ALU.is_ge,
        fill=0.0, base=512, pattern=[[-1, 640]], channel_multiplier=1),
        [B_("maskband")], [B_("maskband")], tag="mb_2")

    for dt in range(NDT):
        sch.op("gp", lambda e, dt=dt: e.memset(headmask[:, dt, :], 0.0), [],
               [B_("hm", dt)], tag=f"hm_z{dt}")
        sch.op("gp", lambda e, dt=dt: e.memset(
            headmask[0:64, dt, 2 * dt:2 * dt + 1], 1.0),
            [B_("hm", dt)], [B_("hm", dt)], tag=f"hm_a{dt}")
        sch.op("gp", lambda e, dt=dt: e.memset(
            headmask[64:128, dt, 2 * dt + 1:2 * dt + 2], 1.0),
            [B_("hm", dt)], [B_("hm", dt)], tag=f"hm_b{dt}")

    sch.op("gp", lambda e: e.memset(ones1p[:, :], 1.0), [], [B_("ones1p")], tag="ones1p")
    sch.op("gp", lambda e: e.memset(ones1pf[:, :], 1.0), [], [B_("ones1pf")], tag="ones1pf")
    sch.op("gp", lambda e: e.memset(kaug[64:65, 0, :], 1.0), [], [B_("kaug_ones", 0)], tag="kones0")
    sch.op("gp", lambda e: e.memset(kaug[64:65, 1, :], 1.0), [], [B_("kaug_ones", 1)], tag="kones1")
    for tt in range(NTT):
        sch.op("gp", lambda e, tt=tt: e.memset(vp[:, tt, :, 64:65], 1.0), [],
               [B_("vp_ones", tt)], tag=f"vpones{tt}")

    for src_, dst, tag in ((g1, g1b1_sb[:, 0:NDT], "g1"),
                           (b1ln, g1b1_sb[:, NDT:2 * NDT], "b1ln"),
                           (bias1, bias1_sb[:, :], "bias1")):
        dma("sp", dst, src_.rearrange("(a p) -> p a", p=128), [], [B_(tag)],
            tag="ld_" + tag, slow=True, q=("sm",), barrier=True)
    dma("sp", b2row[:, :], bias2[None, :], [], [B_("b2row")],
        tag="ld_b2row", q=("sm",), barrier=True)
    dma("sp", bt2row[:, :], b2ln[None, :], [], [B_("bt2row")],
        tag="ld_bt2row", q=("sm",), barrier=True)
    dma("sp", g2row, g2[None, :], [],
        [B_("g2row")], tag="ld_g2row", q=("sm",), barrier=True)
    dma("sp", g1row, g1[None, :], [], [B_("g1row")],
        tag="ld_g1row", q=("sm",), barrier=True)
    dma("sp", b1rowT[:, :], b1ln[None, :], [], [B_("b1row")],
        tag="ld_b1row", q=("sm",), barrier=True)
    dma("sp", fbT_f32[:, :], fastb.rearrange("a b -> b a"), [], [B_("fbT32")],
        tag="ld_fbT", slow=True, q=("sm",), barrier=True)
    dma("sp", gate_f[:, :], gate, [], [B_("gate_f")],
        tag="ld_gate", q=("sm",), barrier=True)
    dma("sp", prep_f32[0:16, 2048:3072], vv, [], [EB(4), EB(5)],
        tag="ld_v", q=("sm",), barrier=True)
    dma("sp", prep_f32[:, 3072:3584], uu.rearrange("(p a) r -> p (a r)", p=128),
        [], [EB(6)], tag="ld_u", q=("sm",), barrier=True)

    sch.op("dve", lambda e: e.tensor_copy(out=fbT_sb[:, :], in_=fbT_f32[:, :]),
           [B_("fbT32")], [B_("fbT")], tag="fbT_cast")

    for dt in range(NDT):
        sl = dt % 4
        sch.op("pe", lambda e, dt=dt, sl=sl: e.transpose(
            bank(0, 16 * sl, 16 * sl + 16),
            prep_f32[0:16, 2048 + 128 * dt:2048 + 128 * (dt + 1)],
            ident_f32[0:16, 0:16]),
            [EB(4), EB(5), B_("ident_f32")], [PB(0)], tag=f"vT_t{dt}")
        sch.op("dve", lambda e, dt=dt, sl=sl: e.tensor_copy(
            out=vT_sb[:, dt * 16:(dt + 1) * 16], in_=bank(0, 16 * sl, 16 * sl + 16)),
            [PB(0)], [B_("vT", dt)], tag=f"vT_e{dt}")
    for a in range(32):
        sl = a % 4
        sch.op("pe", lambda e, a=a, sl=sl: e.transpose(
            bank(1, 128 * sl, 128 * sl + 128, part=16),
            prep_f32[:, 3072 + 16 * a:3072 + 16 * (a + 1)], ident_f32[:, :]),
            [EB(6), B_("ident_f32")], [PB(1)], tag=f"uT_t{a}")
        sch.op("dve", lambda e, a=a, sl=sl: e.tensor_copy(
            out=uTg_sb.ap().rearrange("r (f a2) -> r f a2", a2=32)[:, :, a],
            in_=bank(1, 128 * sl, 128 * sl + 128, part=16)),
            [PB(1)], [B_("uTg")], tag=f"uT_e{a}")
    sch.op("pe", lambda e: e.matmul(bank(0, 256, 257, part=16),
                                    ones1pf[0:1, 0:16], gate_f[0:1, 0:1],
                                    start=True, stop=True),
           [B_("ones1pf"), B_("gate_f")], [PB(0)], tag="gate_mm")
    sch.op("dve", lambda e: e.tensor_copy(out=gate16[:, :],
                                          in_=bank(0, 256, 257, part=16)),
           [PB(0)], [B_("gate16")], tag="gate_evac")
    sch.op("dve", lambda e: e.tensor_scalar_mul(uTg_sb[:, :], uTg_sb[:, :],
                                                gate16[0:16, 0:1]),
           [B_("uTg"), B_("gate16")], [B_("uTg")], tag="uTg_scale")

    for rowsrc, dstbc, nm2 in ((g1row, g1bc, "g1bc"),):
        for half in range(2):
            sch.op("pe", lambda e, r=rowsrc, h=half: e.matmul(
                bank(2 + h), ones1pf[0:1, :], r[0:1, 512 * h:512 * (h + 1)],
                start=True, stop=True),
                [B_("ones1pf"), B_(nm2[0:2] + "row")], [PB(2 + half)],
                tag=f"{nm2}_mm{half}")
            sch.op("dve", lambda e, d=dstbc, h=half: e.tensor_copy(
                out=d[:, 512 * h:512 * (h + 1)], in_=bank(2 + h)),
                [PB(2 + half)], [B_(nm2)], tag=f"{nm2}_e{half}")
    for half in range(2):
        sch.op("pe", lambda e, h=half: e.matmul(
            bank(2 + h), ones1pf[0:1, :], g2row[0:1, 512 * h:512 * (h + 1)],
            start=True, stop=True),
            [B_("ones1pf"), B_("g2row")], [PB(2 + half)], tag=f"g2bc_mm{half}")
        sch.op("dve", lambda e, h=half: e.tensor_copy(
            out=g2bc[:, 512 * h:512 * (h + 1)], in_=bank(2 + h)),
            [PB(2 + half)], [B_("g2bc")], tag=f"g2bc_evac{half}")

    thunks_w12 = []
    w1r = w1.rearrange("(dt p) f -> p dt f", p=128)

    def _w1_step(fb):
        rb = fb % 2
        rowdep = ([B_("g1row")] if fb == 0 else
                  [B_("g2row")] if fb == 1 else [])
        dma("sp", prep_f32[:, 1024 * rb:1024 * (rb + 1)].rearrange(
            "p (a b) -> p a b", a=NDT),
            w1r[:, :, 128 * fb:128 * (fb + 1)],
            [], [EB(2 * rb), EB(2 * rb + 1)] + rowdep,
            tag=f"w1ld{fb}", q=("wld", rb))
        sch.op("sc", lambda e, rb=rb: e.activation(
            out=wring[:, rb, :], in_=prep_f32[:, 1024 * rb:1024 * (rb + 1)],
            func=AF.Identity),
            [EB(2 * rb), EB(2 * rb + 1)], [B_("wring", rb)], tag=f"w1c{fb}")
        dma("sc", st_w1[fb], wring[:, rb, :], [B_("wring", rb)],
            [B_("st_w1", fb)], tag=f"w1st{fb}", q=("stg1", rb))

    def _w2_step(fb):
        rb = fb % 2
        dma("sp", prep_f32[:, 1024 * (2 + rb):1024 * (3 + rb)],
            w2[128 * fb:128 * (fb + 1), :],
            [], [EB(4 + 2 * rb), EB(5 + 2 * rb)],
            tag=f"w2ld{fb}", q=("wld", 2 + rb))
        sch.op("sc", lambda e, rb=rb: e.activation(
            out=wring[:, 2 + rb, :], in_=prep_f32[:, 1024 * (2 + rb):1024 * (3 + rb)],
            func=AF.Identity),
            [EB(4 + 2 * rb), EB(5 + 2 * rb)], [B_("wring", 2 + rb)],
            tag=f"w2c{fb}")
        dma("sc", st_w2[fb], wring[:, 2 + rb, :], [B_("wring", 2 + rb)],
            [B_("st_w2", fb)], tag=f"w2st{fb}", q=("stg2", rb))

    for fb in range(NFB):
        thunks_w12.append(lambda fb=fb: _w1_step(fb))
        thunks_w12.append(lambda fb=fb: _w2_step(fb))

    GRID = 32768.0
    RBIG = 8388608.0
    _CONSTS = (GRID, 1.0 / GRID, RBIG, -256.0, 2.0 * D, 2.0 * DFF)
    for _i, _v in enumerate(_CONSTS):
        nc.const_aps.register(F32, _v, constcol[:, _i:_i + 1])
        sch.op("gp", lambda e, _i=_i, _v=_v: e.memset(constcol[:, _i:_i + 1], _v),
               [], [B_("constcol")], tag=f"const{_i}")
    Q1 = prep_f32[:, 1024:2048]
    Q2 = prep_f32[:, 2048:3072]
    Q3 = prep_f32[:, 3072:4096]

    def emit_split_sum(wq, wbufs, shi, slo, first, statbuf, base, tag):
        A, HQ, LQ = E8(base + 1), E8(base + 2), E8(base + 3)
        ab, hb, lb = EB(base + 1), EB(base + 2), EB(base + 3)
        sch.op("sc", lambda e: e.activation(out=A, in_=wq, func=AF.Abs),
               wbufs, [ab], tag=tag + "a")
        sch.op("sc", lambda e: e.activation(out=HQ, in_=A, func=AF.Identity,
                                            scale=GRID, bias=RBIG),
               [ab, B_("constcol")], [hb], tag=tag + "h1")
        sch.op("sc", lambda e: e.activation(out=HQ, in_=HQ, func=AF.Identity,
                                            scale=1.0 / GRID, bias=-256.0),
               [hb, B_("constcol")], [hb], tag=tag + "h2")
        sch.op("dve", lambda e: e.tensor_sub(LQ, A, HQ),
               [ab, hb], [lb], tag=tag + "lo")
        if first:
            def _r(e):
                e.reduce_sum(shi, HQ, axis=mybir.AxisListType.X)
                e.drain()
                return e.reduce_sum(slo, LQ, axis=mybir.AxisListType.X)
            sch.op("dve", _r, [hb, lb], [statbuf], tag=tag + "r")
        else:
            sx = 6 if base == 0 else 7

            def _r(e, sx=sx):
                e.reduce_sum(prep_vec[:, sx:sx + 1], HQ,
                             axis=mybir.AxisListType.X)
                e.drain()
                e.tensor_scalar(shi, shi, prep_vec[:, sx:sx + 1], 0.0,
                                ALU.add, ALU.add)
                e.drain()
                e.reduce_sum(prep_vec[:, sx:sx + 1], LQ,
                             axis=mybir.AxisListType.X)
                e.drain()
                return e.tensor_scalar(slo, slo, prep_vec[:, sx:sx + 1], 0.0,
                                       ALU.add, ALU.add)
            sch.op("dve", _r, [hb, lb, statbuf],
                   [statbuf, B_("prep_vec")], tag=tag + "r")

    def emit_tern(wq, wbufs, shi, negshi, slo, scale, dst, dstbufs, statbuf,
                  base, tag, n):
        S1, S2, OT = E8(base + 1), E8(base + 2), E8(base + 3)
        s1b, s2b, ob = EB(base + 1), EB(base + 2), EB(base + 3)
        sch.op("sc", lambda e: e.activation(out=S1, in_=wq, func=AF.Identity,
                                            scale=2.0 * n, bias=negshi),
               wbufs + [statbuf, B_("constcol")], [s1b], tag=tag + "s1")
        sch.op("sc", lambda e: e.activation(out=S2, in_=wq, func=AF.Identity,
                                            scale=2.0 * n, bias=shi),
               wbufs + [statbuf, B_("constcol")], [s2b], tag=tag + "s2")
        sch.op("dve", lambda e: e.tensor_scalar(S1, S1, slo, 0.0,
                                                ALU.subtract, ALU.is_gt),
               [s1b, statbuf], [s1b], tag=tag + "g")
        sch.op("dve", lambda e: e.tensor_scalar(S2, S2, slo, 0.0,
                                                ALU.add, ALU.is_lt),
               [s2b, statbuf], [s2b], tag=tag + "l")

        def _wt(e):
            e.tensor_sub(OT, S1, S2)
            e.drain()
            return e.tensor_scalar(dst, OT, scale, 0.0, ALU.mult, ALU.add)
        sch.op("dve", _wt, [s1b, s2b, statbuf], [ob] + dstbufs, tag=tag + "w")

    thunks_fc = []

    def _fc_step(ft):
        for hf in range(2):
            base = 4 * hf
            dma("sp", E8(base), wfc[128 * ft:128 * (ft + 1),
                                    512 * hf:512 * (hf + 1)],
                [], [EB(base)], tag=f"fc_ld{ft}_{hf}", q=("prep", hf))
            emit_split_sum(E8(base), [EB(base)], prep_vec[:, 0:1],
                           prep_vec[:, 1:2], hf == 0, B_("prep_vec"),
                           base, f"fc{ft}_{hf}")

        def _sc2(e):
            e.tensor_scalar(prep_vec[:, 3:4], prep_vec[:, 0:1],
                            prep_vec[:, 1:2], 0.0, ALU.add, ALU.add)
            e.tensor_scalar(prep_vec[:, 4:5], prep_vec[:, 0:1], -1.0, 0.0,
                            ALU.mult, ALU.add)
            e.drain()
            return e.tensor_scalar(prep_vec[:, 3:4], prep_vec[:, 3:4],
                                   1.0 / D, 1e-5, ALU.mult, ALU.max)

        sch.op("dve", _sc2, [B_("prep_vec")], [B_("prep_vec")],
               tag=f"fc{ft}_sc")
        for hf in range(2):
            base = 4 * hf
            emit_tern(E8(base), [EB(base)], prep_vec[:, 0:1],
                      prep_vec[:, 4:5], prep_vec[:, 1:2], prep_vec[:, 3:4],
                      prep_a[:, 512 * hf:512 * (hf + 1)],
                      [B_("prep_a", hf)], B_("prep_vec"), base,
                      f"fc{ft}t{hf}", D)
        for dt in range(NDT):
            sch.op("pe", lambda e, dt=dt: e.transpose(
                bank_bf(5, 128 * dt, 128 * (dt + 1)),
                prep_a[:, 128 * dt:128 * (dt + 1)], ident_bf[:, :]),
                [B_("prep_a", dt // 4), B_("ident_bf")], [PB(5)],
                tag=f"fcT{ft}_{dt}")
        sch.op("dve", lambda e: e.tensor_copy(
            out=prep_st[:, :], in_=bank_bf(5, 0, 1024)),
            [PB(5)], [B_("prep_st", d2) for d2 in range(NDT)],
            tag=f"fcTe{ft}")
        dma("sc", st_wfcT[ft], prep_st[:, :],
            [B_("prep_st", d2) for d2 in range(NDT)],
            [B_("st_wfcT", ft)], tag=f"fc_st{ft}", q=("stgfc",))

    for ft in range(NFB):
        thunks_fc.append(lambda ft=ft: _fc_step(ft))

    thunks_pr1 = []

    def _pr1_step(dt):
        for qq in range(4):
            for hf in range(2):
                base = 4 * hf
                c0 = 1024 * qq + 512 * hf
                dma("sp", E8(base), wpr[128 * dt:128 * (dt + 1), c0:c0 + 512],
                    [], [EB(base)], tag=f"pr_ld{dt}_{qq}_{hf}",
                    q=("prep", hf))
                emit_split_sum(E8(base), [EB(base)],
                               prscale[:, NDT + dt:NDT + dt + 1],
                               prscale[:, 2 * NDT + dt:2 * NDT + dt + 1],
                               qq == 0 and hf == 0, B_("prscale"),
                               base, f"pr{dt}_{qq}_{hf}")

        def _psc(e, dt=dt):
            e.tensor_scalar(prscale[:, dt:dt + 1],
                            prscale[:, NDT + dt:NDT + dt + 1],
                            prscale[:, 2 * NDT + dt:2 * NDT + dt + 1],
                            0.0, ALU.add, ALU.add)
            e.tensor_scalar(prscale[:, 3 * NDT + dt:3 * NDT + dt + 1],
                            prscale[:, NDT + dt:NDT + dt + 1], -1.0, 0.0,
                            ALU.mult, ALU.add)
            e.drain()
            return e.tensor_scalar(prscale[:, dt:dt + 1], prscale[:, dt:dt + 1],
                                   1.0 / DFF, 1e-5, ALU.mult, ALU.max)

        sch.op("dve", _psc, [B_("prscale")], [B_("prscale")],
               tag=f"pr_sc{dt}")

    for dt in range(NDT):
        thunks_pr1.append(lambda dt=dt: _pr1_step(dt))

    thunks_pr2 = []

    def _pr2_i(dt, i8):
        par = i8 % 2
        base = 4 * par
        c0 = 512 * i8
        dma("sp", E8(base), wpr[128 * dt:128 * (dt + 1), c0:c0 + 512],
            [], [EB(base)], tag=f"prf_ld{dt}_{i8}", q=("prep", par))
        emit_tern(E8(base), [EB(base)],
                  prscale[:, NDT + dt:NDT + dt + 1],
                  prscale[:, 3 * NDT + dt:3 * NDT + dt + 1],
                  prscale[:, 2 * NDT + dt:2 * NDT + dt + 1],
                  prscale[:, dt:dt + 1],
                  prep_a[:, 512 * par:512 * (par + 1)],
                  [B_("prep_a", par)], B_("prscale"), base,
                  f"prf{dt}_{i8}", DFF)
        for k in range(4):
            sch.op("pe", lambda e, k=k, par=par: e.transpose(
                bank_bf(5, 512 * par + 128 * k, 512 * par + 128 * (k + 1)),
                prep_a[:, 512 * par + 128 * k:512 * par + 128 * (k + 1)],
                ident_bf[:, :]),
                [B_("prep_a", par), B_("ident_bf")], [PB(5)],
                tag=f"prfT{dt}_{i8}_{k}")
        sch.op("dve", lambda e, par=par: e.tensor_copy(
            out=prep_st[:, 512 * par:512 * (par + 1)],
            in_=bank_bf(5, 512 * par, 512 * (par + 1))),
            [PB(5)], [B_("prep_st", par)], tag=f"prfTe{dt}_{i8}")
        for k in range(4):
            fc = 4 * i8 + k
            dma("sc", st_wprT[fc][:, 128 * dt:128 * (dt + 1)],
                prep_st[:, 512 * par + 128 * k:512 * par + 128 * (k + 1)],
                [B_("prep_st", par)],
                [B_("st_wprT", fc, dt)], tag=f"prst{dt}_{fc}",
                q=("stgpr", par))

    for dt in range(NDT):
        for i8 in range(8):
            thunks_pr2.append(lambda dt=dt, i8=i8: _pr2_i(dt, i8))

    AXX = mybir.AxisListType.X

    for c in range(NCHUNK):
        w0 = c * TC
        for tt in range(NTT):
            xb = tt % 2
            dma("sp", xbuf[:, xb, :], xh[w0 + 128 * tt: w0 + 128 * (tt + 1), :],
                [], [B_("xbuf", xb)], tag=f"c{c}xld{tt}", q=("x", xb))
            xt = xbuf[:, xb, :]
            sp_ = 8 * tt

            def _stats(e, xt=xt, sp_=sp_):
                e.reduce_sum(stat[:, sp_ + 0:sp_ + 1], xt, axis=AXX)
                e.scalar_tensor_tensor(out=sqscr[:, :], in0=xt, scalar=1.0,
                                       in1=xt, op0=ALU.mult, op1=ALU.mult,
                                       accum_out=stat[:, sp_ + 1:sp_ + 2])
                e.drain()
                e.tensor_scalar(stat[:, sp_ + 2:sp_ + 3], stat[:, sp_ + 0:sp_ + 1],
                                1.0 / D, None, ALU.mult)
                e.drain()
                e.tensor_scalar(stat[:, sp_ + 3:sp_ + 4], stat[:, sp_ + 2:sp_ + 3],
                                stat[:, sp_ + 2:sp_ + 3], EPS,
                                ALU.mult, ALU.subtract)
                e.drain()
                return e.scalar_tensor_tensor(
                    out=stat[:, sp_ + 4:sp_ + 5], in0=stat[:, sp_ + 1:sp_ + 2],
                    scalar=1.0 / D, in1=stat[:, sp_ + 3:sp_ + 4],
                    op0=ALU.mult, op1=ALU.subtract)

            sch.op("dve", _stats, [B_("xbuf", xb)],
                   [B_("stat", tt), B_("sqh", 0), B_("sqh", 1)],
                   tag=f"c{c}stats{tt}")
            sch.op("sc", lambda e, sp_=sp_: e.activation(
                out=stat[:, sp_ + 5:sp_ + 6], in_=stat[:, sp_ + 4:sp_ + 5],
                func=AF.Sqrt),
                [B_("stat", tt)], [B_("stat", tt)], tag=f"c{c}sqrt{tt}")

            def _rstd(e, sp_=sp_):
                e.reciprocal(stat[:, sp_ + 6:sp_ + 7], stat[:, sp_ + 5:sp_ + 6])
                e.drain()
                return e.tensor_scalar(stat[:, sp_ + 7:sp_ + 8],
                                       stat[:, sp_ + 2:sp_ + 3],
                                       stat[:, sp_ + 6:sp_ + 7],
                                       -1.0, ALU.mult, ALU.mult)

            sch.op("dve", _rstd, [B_("stat", tt)], [B_("stat", tt)],
                   tag=f"c{c}rstd{tt}")
            sch.op("sc", lambda e, xt=xt, tt=tt, sp_=sp_: e.activation(
                out=vp[:, tt, :, 0:64],
                in_=xt.rearrange("p (h d) -> p h d", h=H),
                func=AF.Identity, bias=stat[:, sp_ + 7:sp_ + 8],
                scale=stat[:, sp_ + 6:sp_ + 7]),
                [B_("xbuf", xb), B_("stat", tt)], [B_("vp", tt)],
                tag=f"c{c}vp{tt}")
        if c == 0:
            for t_ in thunks_w12:
                t_()
        for tt in range(NTT):
            for g in range(2):
                pb = 4 + g
                for k in range(4):
                    dt = g * 4 + k
                    for hh in (2 * dt, 2 * dt + 1):
                        p0 = (hh % 2) * 64
                        sch.op("pe", lambda e, tt=tt, pb=pb, k=k, hh=hh, p0=p0:
                               e.transpose(
                                   psum_bf[p0:p0 + 64,
                                           1024 * pb + 128 * k:1024 * pb + 128 * (k + 1)],
                                   vp[:, tt, hh, 0:64], ident_bf[:, :]),
                               [B_("vp", tt), B_("ident_bf")], [PB(pb)],
                               tag=f"c{c}xT{tt}_{hh}")
                sch.op("dve", lambda e, tt=tt, g=g, pb=pb: e.tensor_copy(
                    out=xnT[:, 4 * g:4 * (g + 1), 128 * tt:128 * (tt + 1)],
                    in_=bank_bf(pb, 0, 512).rearrange("p (a b) -> p a b", a=4)),
                    [PB(pb)], [B_("xnT", tt, g)], tag=f"c{c}xTe{tt}_{g}")
            if 2 <= tt < 2 + NQT:
                qi = tt - 2
                for dt in range(NDT):
                    sch.op("sc", lambda e, dt=dt, tt=tt:
                           e.activation(out=xnT2[:, dt % 2, :],
                                        in_=xnT[:, dt, 128 * tt:128 * (tt + 1)],
                                        func=AF.Square),
                           [B_("xnT", tt, dt // 4)], [B_("xnT2", dt % 2)],
                           tag=f"c{c}sq{tt}_{dt}")
                    sch.op("pe", lambda e, dt=dt, qi=qi: e.matmul(
                        bank(6, 128 * qi, 128 * (qi + 1), part=16),
                        headmask[:, dt, :], xnT2[:, dt % 2, :],
                        start=(dt == 0), stop=(dt == NDT - 1)),
                        [B_("hm", dt), B_("xnT2", dt % 2)], [PB(6)],
                        tag=f"c{c}mm{tt}_{dt}")
                sch.op("dve", lambda e, qi=qi: e.tensor_scalar_mul(
                    m16[:, 128 * qi:128 * (qi + 1)],
                    bank(6, 128 * qi, 128 * (qi + 1), part=16), -1.0),
                    [PB(6)], [B_("m16", qi)], tag=f"c{c}me{tt}")

        NKB = WINT // 128
        KB_ORDER = [3] + [k for k in range(NKB) if k != 3]

        def emit_aug(h):
            hs = h % 2
            dt_h, r0 = h // 2, (h % 2) * 64
            xnT_keys = [B_("xnT", tt2, dt_h // 4) for tt2 in range(NTT)]
            dma("sc", kaug[0:64, hs, :], xnT[r0:r0 + 64, dt_h, :],
                xnT_keys, [B_("kaug", hs)], tag=f"c{c}h{h}kaug",
                q=("kaug", hs))
            dma("sc", qaug[0:64, hs, :], xnT[r0:r0 + 64, dt_h, HALO:HALO + TC],
                xnT_keys, [B_("qaug", hs)], tag=f"c{c}h{h}qaug",
                q=("qaug", hs))
            dma("sc", qaug[64:65, hs, :], m16[h:h + 1, :],
                [B_("m16", qi2) for qi2 in range(NQT)], [B_("qaugm", hs)],
                tag=f"c{c}h{h}qm", q=("qaugm", hs))

        emit_aug(0)
        for h in range(H):
            hs = h % 2
            avb = 6 + (h % 2)
            ptb = h % 2

            def emit_post(idx, h=h, hs=hs, avb=avb):
                ikb = KB_ORDER[idx]
                qlo = max(ikb * 128 - 256, HALO)
                qhi = min(ikb * 128 + 384, HALO + TC)
                ql, qh = qlo - HALO, qhi - HALO
                w = qh - ql
                sbk = 2 + (idx % 3)
                ps = idx % 3
                sch.op("sc", lambda e, sbk=sbk, w=w, ps=ps: e.activation(
                    out=psb[:, ps, 0:w], in_=bank(sbk, 0, w), func=AF.Exp),
                    [PB(sbk)], [B_("psb", ps)], tag=f"c{c}h{h}e{ikb}")
                j0 = qlo - (ikb * 128 - 256)
                for jr0, jr1 in ((0, 128), (513, 640)):
                    lo = max(jr0, j0)
                    hi = min(jr1, j0 + w)
                    if lo < hi:
                        sch.op("gp", lambda e, ps=ps, lo=lo, hi=hi, j0=j0:
                               e.tensor_mul(psb[:, ps, lo - j0:hi - j0],
                                            psb[:, ps, lo - j0:hi - j0],
                                            maskband[:, lo:hi]),
                               [B_("psb", ps), B_("maskband")], [B_("psb", ps)],
                               tag=f"c{c}h{h}m{ikb}_{lo}")
                sch.op("pe", lambda e, ikb=ikb, h=h, ps=ps, ql=ql, qh=qh,
                       avb=avb, w=w: e.matmul(
                           bank(avb, ql, qh, part=65),
                           vp[:, ikb, h, 0:65], psb[:, ps, 0:w],
                           start=(ikb == KB_ORDER[0]),
                           stop=(ikb == KB_ORDER[-1])),
                       [B_("psb", ps), B_("vp", ikb), B_("vp_ones", ikb)],
                       [PB(avb)], tag=f"c{c}h{h}av{ikb}")

            def emit_smm(idx, h=h, hs=hs):
                ikb = KB_ORDER[idx]
                qlo = max(ikb * 128 - 256, HALO)
                qhi = min(ikb * 128 + 384, HALO + TC)
                ql, qh = qlo - HALO, qhi - HALO
                sbk = 2 + (idx % 3)
                sch.op("pe", lambda e, hs=hs, ikb=ikb, ql=ql, qh=qh, sbk=sbk:
                       e.matmul(bank(sbk, 0, qh - ql),
                                kaug[0:65, hs, 128 * ikb:128 * (ikb + 1)],
                                qaug[0:65, hs, ql:qh], start=True, stop=True),
                       [B_("kaug", hs), B_("kaug_ones", hs), B_("qaug", hs),
                        B_("qaugm", hs)],
                       [PB(sbk)], tag=f"c{c}h{h}s{ikb}")

            emit_smm(0)
            emit_smm(1)
            if h + 1 < H:
                emit_aug(h + 1)
            for idx in range(NKB):
                if idx + 2 < NKB:
                    emit_smm(idx + 2)
                emit_post(idx)
            sch.op("dve", lambda e, hs=hs, avb=avb: e.tensor_copy(
                out=avT[0:65, hs, :], in_=bank(avb, 0, 512, part=65)),
                [PB(avb)], [B_("avT", hs)], tag=f"c{c}h{h}avE")
            for tb in range(NQT):
                sch.op("pe", lambda e, hs=hs, tb=tb, ptb=ptb: e.transpose(
                    bank(ptb, 65 * tb, 65 * (tb + 1)),
                    avT[0:65, hs, 128 * tb:128 * (tb + 1)],
                    ident_f32[0:65, 0:65]),
                    [B_("avT", hs), B_("ident_f32")], [PB(ptb)],
                    tag=f"c{c}h{h}T{tb}")
            sch.op("dve", lambda e, ptb=ptb: e.reciprocal(
                out=rz[:, 0:NQT],
                in_=bank(ptb, 0, 65 * NQT).rearrange(
                    "p (s x) -> p s x", x=65)[:, :, 64]),
                [PB(ptb)], [B_("rz")], tag=f"c{c}h{h}rz")
            for tb in range(NQT):
                sch.op("dve", lambda e, h=h, tb=tb, ptb=ptb:
                       e.scalar_tensor_tensor(
                           out=ln1in[:, tb, 64 * h:64 * (h + 1)],
                           in0=bank(ptb, 65 * tb, 65 * tb + 64),
                           scalar=rz[:, tb:tb + 1],
                           in1=vp[:, 2 + tb, h, 0:64],
                           op0=ALU.mult, op1=ALU.add),
                       [PB(ptb), B_("rz"), B_("vp", 2 + tb),
                        B_("vp_ones", 2 + tb)],
                       [B_("ln1", tb, h)], tag=f"c{c}h{h}j{tb}")

        if c == 0:
            for t_ in thunks_fc:
                t_()
        if "ln1in" in dbg:
            for tb in range(NQT):
                dma("sc", dbg["ln1in"][c, tb], ln1in[:, tb, :],
                    [B_("ln1", tb, h2) for h2 in range(H)],
                    [B_("dbgl1", c, tb)], tag=f"dbgl1{c}_{tb}",
                    q=("dbg", c), barrier=True)

        def ln_stats(x_ap, reads, par, tag):
            sp_ = 8 * par

            def _st(e, x_ap=x_ap, sp_=sp_):
                e.reduce_sum(stat[:, sp_ + 0:sp_ + 1], x_ap, axis=AXX)
                e.scalar_tensor_tensor(out=sqscr[:, :], in0=x_ap, scalar=1.0,
                                       in1=x_ap, op0=ALU.mult, op1=ALU.mult,
                                       accum_out=stat[:, sp_ + 1:sp_ + 2])
                e.drain()
                e.tensor_scalar(stat[:, sp_ + 2:sp_ + 3],
                                stat[:, sp_ + 0:sp_ + 1], 1.0 / D, None,
                                ALU.mult)
                e.drain()
                e.tensor_scalar(stat[:, sp_ + 3:sp_ + 4],
                                stat[:, sp_ + 2:sp_ + 3],
                                stat[:, sp_ + 2:sp_ + 3], EPS,
                                ALU.mult, ALU.subtract)
                e.drain()
                return e.scalar_tensor_tensor(
                    out=stat[:, sp_ + 4:sp_ + 5], in0=stat[:, sp_ + 1:sp_ + 2],
                    scalar=1.0 / D, in1=stat[:, sp_ + 3:sp_ + 4],
                    op0=ALU.mult, op1=ALU.subtract)

            sch.op("dve", _st, reads, [B_("stat", par), B_("sqh", 0), B_("sqh", 1)],
                   tag=tag + "_st")
            sch.op("sc", lambda e, sp_=sp_: e.activation(
                out=stat[:, sp_ + 5:sp_ + 6], in_=stat[:, sp_ + 4:sp_ + 5],
                func=AF.Sqrt),
                [B_("stat", par)], [B_("stat", par)], tag=tag + "_sq")
            sch.op("dve", lambda e, sp_=sp_: e.reciprocal(
                stat[:, sp_ + 6:sp_ + 7], stat[:, sp_ + 5:sp_ + 6]),
                [B_("stat", par)], [B_("stat", par)], tag=tag + "_r")
            return stat[:, sp_ + 2:sp_ + 3], stat[:, sp_ + 6:sp_ + 7]

        for tb in range(NQT):
            l1keys = [B_("ln1", tb, h2) for h2 in range(H)]
            mu_ap, rstd_ap = ln_stats(ln1in[:, tb, :], l1keys, tb,
                                      f"c{c}l1s{tb}")
            sch.op("dve", lambda e, tb=tb, mu_ap=mu_ap, rstd_ap=rstd_ap:
                   e.tensor_scalar(t1s[:, tb, :], ln1in[:, tb, :],
                                   mu_ap, rstd_ap, ALU.subtract, ALU.mult),
                   l1keys + [B_("stat", tb)], [B_("t1s", tb)],
                   tag=f"c{c}t1_{tb}")
        for tb in range(NQT):
            sch.op("dve", lambda e, tb=tb: e.tensor_mul(
                x1tok[:, tb, :], t1s[:, tb, :], g1bc[:, :]),
                [B_("t1s", tb), B_("g1bc")], [B_("x1tok", tb)],
                tag=f"c{c}xg{tb}")
        for dt in range(NDT):
            pb = 4 + dt % 2
            for tb in range(NQT):
                sch.op("pe", lambda e, tb=tb, dt=dt, pb=pb: e.transpose(
                    bank_bf(pb, 128 * tb, 128 * (tb + 1)),
                    t1s[:, tb, 128 * dt:128 * (dt + 1)], ident_bf[:, :]),
                    [B_("t1s", tb), B_("ident_bf")], [PB(pb)],
                    tag=f"c{c}x1T{tb}_{dt}")
            sch.op("dve", lambda e, dt=dt, pb=pb: e.tensor_scalar(
                x1T[:, dt, :], bank_bf(pb, 0, 512),
                g1b1_sb[:, dt:dt + 1],
                g1b1_sb[:, NDT + dt:NDT + dt + 1],
                ALU.mult, ALU.add),
                [PB(pb), B_("g1"), B_("b1ln")],
                [B_("x1T", tb2, dt) for tb2 in range(NQT)],
                tag=f"c{c}x1Te{dt}")

        if c == 0:
            for t_ in thunks_pr1:
                t_()
        for fb in range(NFB):
            rs = fb % 4
            dma("sp", wring[:, rs, :], st_w1[fb],
                [B_("st_w1", fb)], [B_("wring", rs)],
                tag=f"c{c}w1l{fb}", q=("ring", rs))
            hb = 4 + (fb % 2)
            for dt in range(NDT):
                sch.op("pe", lambda e, rs=rs, dt=dt, hb=hb: e.matmul(
                    bank(hb), wring[:, rs, 128 * dt:128 * (dt + 1)],
                    x1T[:, dt, :], start=(dt == 0), stop=(dt == NDT - 1)),
                    [B_("wring", rs)] + [B_("x1T", tb2, dt)
                                         for tb2 in range(NQT)],
                    [PB(hb)], tag=f"c{c}ff1_{fb}_{dt}")
            sch.op("sc", lambda e, fb=fb, hb=hb: e.activation(
                out=pool[:, fb, :], in_=bank(hb), func=AF.Relu,
                bias=bias1_sb[:, fb:fb + 1]),
                [PB(hb), B_("bias1")], [B_("pool", fb)],
                tag=f"c{c}relu{fb}")

        for tb in range(NQT):
            for dg in range(2):
                bi = 2 * tb + dg
                sch.op("pe", lambda e, bi=bi, dg=dg: e.matmul(
                    bank(bi), ones1pf[0:1, :],
                    b2row[0:1, 512 * dg:512 * (dg + 1)],
                    start=True, stop=False),
                    [B_("ones1pf"), B_("b2row")], [PB(bi)],
                    tag=f"c{c}b2_{tb}_{dg}")
                sch.op("pe", lambda e, bi=bi, dg=dg: e.matmul(
                    bank(bi), ones1pf[0:1, :],
                    b1rowT[0:1, 512 * dg:512 * (dg + 1)],
                    start=False, stop=False),
                    [B_("ones1pf"), B_("b1row")], [PB(bi)],
                    tag=f"c{c}bb1_{tb}_{dg}")
        for fb in range(NFB):
            rs = fb % 4
            dma("sp", wring[:, rs, :], st_w2[fb],
                [B_("st_w2", fb)], [B_("wring", rs)],
                tag=f"c{c}w2l{fb}", q=("ring", rs))
            for tb in range(NQT):
                for dg in range(2):
                    bi = 2 * tb + dg
                    sch.op("pe", lambda e, rs=rs, fb=fb, tb=tb, dg=dg, bi=bi:
                           e.matmul(bank(bi),
                                    pool[:, fb, 128 * tb:128 * (tb + 1)],
                                    wring[:, rs, 512 * dg:512 * (dg + 1)],
                                    start=False, stop=(fb == NFB - 1)),
                           [B_("pool", fb), B_("wring", rs)], [PB(bi)],
                           tag=f"c{c}ff2_{fb}_{tb}_{dg}")

        for tb in range(NQT):
            l1keys = [B_("ln1", tb, h2) for h2 in range(H)]
            sch.op("dve", lambda e, tb=tb: e.scalar_tensor_tensor(
                out=ln1in[:, tb, :], in0=bank(2 * tb, 0, 512 * 2),
                scalar=1.0, in1=x1tok[:, tb, :],
                op0=ALU.bypass, op1=ALU.add),
                [PB(2 * tb), PB(2 * tb + 1), B_("x1tok", tb)],
                l1keys, tag=f"c{c}l2j{tb}")
            mu2, rstd2 = ln_stats(ln1in[:, tb, :], l1keys, 4 + tb,
                                  f"c{c}l2s{tb}")
            sch.op("dve", lambda e, tb=tb, mu2=mu2, rstd2=rstd2:
                   e.tensor_scalar(sqscr[:, :], ln1in[:, tb, :],
                                   mu2, rstd2, ALU.subtract, ALU.mult),
                   l1keys + [B_("stat", 4 + tb)], [B_("sqh", 0), B_("sqh", 1)],
                   tag=f"c{c}z2_{tb}")
            sch.op("dve", lambda e, tb=tb: e.tensor_mul(
                t1s[:, tb, :], sqscr[:, :], g2bc[:, :]),
                [B_("sqh", 0), B_("sqh", 1), B_("g2bc")], [B_("t1s", tb)],
                tag=f"c{c}g2z{tb}")

        if c == 0:
            for t_ in thunks_pr2:
                t_()
        for dt in range(NDT):
            sch.op("pe", lambda e, dt=dt: e.matmul(
                bank(4, 0, 512, part=16), vT_sb[:, 16 * dt:16 * (dt + 1)],
                xnT[:, dt, HALO:HALO + TC],
                start=(dt == 0), stop=(dt == NDT - 1)),
                [B_("vT", dt)] + [B_("xnT", tt2, dt // 4)
                                  for tt2 in range(NTT)],
                [PB(4)], tag=f"c{c}t1T_{dt}")
        sch.op("dve", lambda e: e.tensor_copy(out=lr_sb[0:16, 0, :],
                                              in_=bank(4, 0, 512, part=16)),
               [PB(4)], [B_("lr", 0)], tag=f"c{c}t1Te")
        sch.op("pe", lambda e: e.matmul(bank(5, 0, 512, part=16),
                                        fbT_sb[:, :], lr_sb[0:16, 0, :],
                                        start=True, stop=True),
               [B_("fbT"), B_("lr", 0)], [PB(5)], tag=f"c{c}t2T")
        sch.op("dve", lambda e: e.tensor_copy(out=lr_sb[0:16, 1, :],
                                              in_=bank(5, 0, 512, part=16)),
               [PB(5)], [B_("lr", 1)], tag=f"c{c}t2Te")
        for fc in range(NFB):
            rs = fc % 4
            dma("sp", wring[:, rs, :], st_wfcT[fc],
                [B_("st_wfcT", fc)], [B_("wring", rs)],
                tag=f"c{c}wfl{fc}", q=("ring", rs))
            hb = 4 + (fc % 2)
            for dt in range(NDT):
                sch.op("pe", lambda e, rs=rs, dt=dt, hb=hb: e.matmul(
                    bank(hb), wring[:, rs, 128 * dt:128 * (dt + 1)],
                    xnT[:, dt, HALO:HALO + TC],
                    start=(dt == 0), stop=False),
                    [B_("wring", rs)] + [B_("xnT", tt2, dt // 4)
                                         for tt2 in range(NTT)],
                    [PB(hb)], tag=f"c{c}fc{fc}_{dt}")
            sch.op("pe", lambda e, fc=fc, hb=hb: e.matmul(
                bank(hb), uTg_sb[0:16, 128 * fc:128 * (fc + 1)],
                lr_sb[0:16, 1, :], start=False, stop=True),
                [B_("uTg"), B_("lr", 1)], [PB(hb)], tag=f"c{c}u{fc}")
            sh = fc % 2
            sch.op("sc", lambda e, hb=hb, sh=sh: e.activation(
                out=sqscr[:, 512 * sh:512 * (sh + 1)], in_=bank(hb),
                func=AF.Relu),
                [PB(hb)], [B_("sqh", sh)], tag=f"c{c}rl2_{fc}")
            sch.op("dve", lambda e, fc=fc, hb=hb, sh=sh: e.tensor_mul(
                pool[:, fc, :], sqscr[:, 512 * sh:512 * (sh + 1)], bank(hb)),
                [PB(hb), B_("sqh", sh)], [B_("pool", fc)], tag=f"c{c}sq2_{fc}")

        for tb in range(NQT):
            for dg in range(2):
                bi = 2 * tb + dg
                sch.op("pe", lambda e, bi=bi, dg=dg: e.matmul(
                    bank(bi), ones1pf[0:1, :],
                    bt2row[0:1, 512 * dg:512 * (dg + 1)],
                    start=True, stop=False),
                    [B_("ones1pf"), B_("bt2row")], [PB(bi)],
                    tag=f"c{c}bt2_{tb}_{dg}")
        for fc in range(NFB):
            rs = fc % 4
            dma("sp", wring[:, rs, :], st_wprT[fc],
                [B_("st_wprT", fc, dt2) for dt2 in range(NDT)],
                [B_("wring", rs)], tag=f"c{c}wpl{fc}", q=("ring", rs))
            for tb in range(NQT):
                for dg in range(2):
                    bi = 2 * tb + dg
                    sch.op("pe", lambda e, rs=rs, fc=fc, tb=tb, dg=dg, bi=bi:
                           e.matmul(bank(bi),
                                    pool[:, fc, 128 * tb:128 * (tb + 1)],
                                    wring[:, rs, 512 * dg:512 * (dg + 1)],
                                    start=False, stop=(fc == NFB - 1)),
                           [B_("pool", fc), B_("wring", rs)], [PB(bi)],
                           tag=f"c{c}pj{fc}_{tb}_{dg}")

        for tb in range(NQT):
            xb2 = tb % 2
            dma("sp", xbuf[:, xb2, :],
                xh[HALO + c * TC + 128 * tb: HALO + c * TC + 128 * (tb + 1), :],
                [], [B_("xbuf", xb2)], tag=f"c{c}xr{tb}", q=("x", xb2))
            l1keys = [B_("ln1", tb, h2) for h2 in range(H)]
            sch.op("dve", lambda e, tb=tb, xb2=xb2: e.scalar_tensor_tensor(
                out=ln1in[:, tb, :], in0=bank(2 * tb, 0, 1024),
                scalar=1.0, in1=xbuf[:, xb2, :],
                op0=ALU.bypass, op1=ALU.add),
                [PB(2 * tb), PB(2 * tb + 1), B_("xbuf", xb2)],
                l1keys, tag=f"c{c}a1_{tb}")
            ob = tb % 2
            sch.op("dve", lambda e, tb=tb, ob=ob: e.tensor_add(
                outbuf[:, ob, :], ln1in[:, tb, :], t1s[:, tb, :]),
                l1keys + [B_("t1s", tb)], [B_("outb", ob)],
                tag=f"c{c}out{tb}")
            dma("sc", y[c * TC + 128 * tb: c * TC + 128 * (tb + 1), :],
                outbuf[:, ob, :], [B_("outb", ob)], [B_("y", c, tb)],
                tag=f"c{c}yst{tb}", q=("out", ob))

        if "vp" in dbg:
            for tt in range(NTT):
                dma("sc", dbg["vp"][c, tt], vp[:, tt, :, :], [B_("vp", tt), B_("vp_ones", tt)],
                    [B_("dbgvp", c, tt)], tag=f"dbgvp{c}_{tt}", q=("dbg", c), barrier=True)
        if "xnT" in dbg:
            for dt in range(NDT):
                dma("sc", dbg["xnT"][c, dt], xnT[:, dt, :],
                    [B_("xnT", tt2, dt // 4) for tt2 in range(NTT)],
                    [B_("dbgxnT", c, dt)], tag=f"dbgxnT{c}_{dt}", q=("dbg", c), barrier=True)
        if "m" in dbg:
            dma("sc", dbg["m"][c], m16[:, :], [B_("m16", qi) for qi in range(NQT)],
                [B_("dbgm", c)], tag=f"dbgm{c}", q=("dbg", c), barrier=True)

    import contextlib as _ctx
    with _ctx.ExitStack() as semstack:
        sems = {e: (semstack.enter_context(nc.semaphore(f"s_{e}"))
                    if e != "sp" else None) for e in ENGS}
        qkeys = sch.qkeys()
        qsems = {k: semstack.enter_context(
            nc.semaphore("q_" + "_".join(str(p) for p in k)))
            for k in qkeys}
        run = sch.emit(nc, sems, qsems)
        qeng = {}
        for o in sch.ops:
            if o.dma:
                qeng[o.q] = o.eng
        qfin = {e: [(qsems[k], sch.final_q[k]) for k in qkeys
                    if qeng[k] == e] for e in ENGS}
        with nc.Block() as blk:

            @blk.tensor
            def _(e):
                run("pe", e)

            @blk.vector
            def _(e):
                run("dve", e)

            @blk.scalar
            def _(e):
                run("sc", e)
                for s, v in qfin["sc"]:
                    e.wait_ge(s, v)

            @blk.sync
            def _(e):
                run("sp", e)
                for s, v in qfin["sp"]:
                    e.wait_ge(s, v)

            @blk.gpsimd
            def _(e):
                run("gp", e)
                for s, v in qfin["gp"]:
                    e.wait_ge(s, v)

    stack.close()
    return nc


_NC_CACHE = {}


def _get_nc():
    key = tuple(DEBUG_TAPS)
    if key not in _NC_CACHE:
        _NC_CACHE[key] = build_nc()
    return _NC_CACHE[key]


def kernel(L0_attn_src, L0_attn_ln1_g, L0_attn_ln1_b, L0_attn_ln2_g,
           L0_attn_ln2_b, L0_attn_W1, L0_attn_b1, L0_attn_W2, L0_attn_b2,
           L0_ffn_fast_b, L0_ffn_w_fc, L0_ffn_w_proj, L0_ffn_u, L0_ffn_v,
           L0_ffn_gate):
    x = np.asarray(L0_attn_src, dtype=np.float32)
    nc = _get_nc()
    common = {
        "w1": np.ascontiguousarray(L0_attn_W1, dtype=np.float32),
        "w2": np.ascontiguousarray(L0_attn_W2, dtype=np.float32),
        "wfc": np.ascontiguousarray(L0_ffn_w_fc, dtype=np.float32),
        "wpr": np.ascontiguousarray(L0_ffn_w_proj, dtype=np.float32),
        "g1": np.ascontiguousarray(L0_attn_ln1_g, dtype=np.float32),
        "b1ln": np.ascontiguousarray(L0_attn_ln1_b, dtype=np.float32),
        "g2": np.ascontiguousarray(L0_attn_ln2_g, dtype=np.float32),
        "b2ln": np.ascontiguousarray(L0_attn_ln2_b, dtype=np.float32),
        "bias1": np.ascontiguousarray(L0_attn_b1, dtype=np.float32),
        "bias2": np.ascontiguousarray(L0_attn_b2, dtype=np.float32),
        "fastb": np.ascontiguousarray(L0_ffn_fast_b, dtype=np.float32),
        "uu": np.ascontiguousarray(L0_ffn_u, dtype=np.float32),
        "vv": np.ascontiguousarray(L0_ffn_v, dtype=np.float32),
        "gate": np.asarray(L0_ffn_gate, dtype=np.float32).reshape(1, 1),
    }
    in_maps = []
    for core in range(NCORES):
        b, half = core // 2, core % 2
        lo = half * TCORE - HALO
        hi = half * TCORE + TCORE + HALO
        xhc = np.zeros((TCORE + 2 * HALO, D), dtype=np.float32)
        s0, s1 = max(lo, 0), min(hi, S)
        xhc[s0 - lo: s1 - lo] = x[b, s0:s1]
        m = dict(common)
        m["xh"] = xhc
        in_maps.append(m)
    res = run_bass_kernel_spmd(nc, in_maps, core_ids=list(range(NCORES)))
    out = np.empty((B, S, D), dtype=np.float32)
    for core in range(NCORES):
        b, half = core // 2, core % 2
        out[b, half * TCORE:(half + 1) * TCORE] = res.results[core]["y"]
    globals()["_LAST_RES"] = res
    return out


# version 24
# speedup vs baseline: 1.0219x; 1.0219x over previous
import numpy as np
import concourse.bass as bass
import concourse.mybir as mybir
from concourse.bass_utils import run_bass_kernel_spmd

F32 = mybir.dt.float32
BF16 = mybir.dt.bfloat16
AF = mybir.ActivationFunctionType
ALU = mybir.AluOpType

B, S, D = 4, 8192, 1024
H, DH, WIN_R = 16, 64, 256
DFF = 4096
EPS = 1e-5
NCORES = 8
TCORE = S // 2
HALO = 256
TC = 512
NCHUNK = TCORE // TC
WINT = TC + 2 * HALO
NTT = WINT // 128
NQT = TC // 128
NFB = DFF // 128
NDT = D // 128

DEBUG_TAPS = []
P1_STAGE = 99

ENGS = ("pe", "dve", "sc", "sp", "gp")


class _Buf:
    __slots__ = ("name", "writer", "readers")

    def __init__(self, name):
        self.name = name
        self.writer = None
        self.readers = []


class _Op:
    __slots__ = ("i", "eng", "fn", "dma", "deps", "inc", "marker", "tag",
                 "q", "barrier", "drain")

    def __init__(self, i, eng, fn, dma, tag, q=None, barrier=False):
        self.i, self.eng, self.fn, self.dma, self.tag = i, eng, fn, dma, tag
        self.q, self.barrier = q, barrier
        self.deps = []
        self.inc = False
        self.marker = None
        self.drain = False


class _Sched:

    def __init__(self):
        self.ops = []

    def op(self, eng, fn, reads=(), writes=(), dma=False, tag="", q=None,
           barrier=False):
        if dma:
            assert q is not None, f"dma op {tag} needs a qsem key"
        o = _Op(len(self.ops), eng, fn, dma, tag, q=q, barrier=barrier)
        deps = set()
        def _edge(w):
            if w.eng != eng or w.dma or dma or eng == "gp":
                deps.add(w.i)
            elif eng != "pe":
                o.drain = True
        for b in reads:
            assert b.writer is not None, f"read of uninitialized buf {b.name} by {tag}"
            _edge(b.writer)
        for b in writes:
            w = b.writer
            if w is not None:
                _edge(w)
            for r in b.readers:
                if r.i in deps:
                    continue
                _edge(r)
        o.deps = sorted(deps)
        for b in reads:
            b.readers.append(o)
        for b in writes:
            b.writer = o
            b.readers = []
        self.ops.append(o)
        return o

    def emit(self, nc, sems, qsems):
        for o in self.ops:
            for d in o.deps:
                dep = self.ops[d]
                if not dep.dma:
                    dep.inc = True
        inc_cnt = {e: 0 for e in ENGS}
        q_cnt = {}
        q_tot = {}
        for o in self.ops:
            if o.dma:
                q_tot[o.q] = q_tot.get(o.q, 0) + 16
        for o in self.ops:
            if o.dma:
                q_cnt[o.q] = q_cnt.get(o.q, 0) + 16
                val = q_tot[o.q] if o.barrier else q_cnt[o.q]
                o.marker = (qsems[o.q], val)
            elif o.inc:
                inc_cnt[o.eng] += 1
                o.marker = (sems[o.eng], inc_cnt[o.eng])
        self.final_q = dict(q_tot)
        self.final_inc = dict(inc_cnt)
        streams = {e: [o for o in self.ops if o.eng == e] for e in ENGS}

        def run_stream(eng_name, e):
            hw = {}
            dirty = [False]
            for o in streams[eng_name]:
                for d in o.deps:
                    m = self.ops[d].marker
                    assert m is not None, (
                        f"op {o.i}({o.tag}) waits on {d}"
                        f"({self.ops[d].tag}) which has no marker")
                    sem, val = m
                    key = id(sem)
                    if hw.get(key, 0) < val:
                        e.wait_ge(sem, val)
                        hw[key] = val
                if o.drain and dirty[0]:
                    e.drain()
                    dirty[0] = False
                inst = o.fn(e)
                dirty[0] = True
                if o.dma:
                    inst.then_inc(qsems[o.q], 16)
                elif o.inc:
                    inst.then_inc(sems[o.eng], 1)
        return run_stream

    def qkeys(self):
        return sorted({o.q for o in self.ops if o.dma}, key=str)


def _ceil(a, b):
    return -(-a // b)


def build_nc():
    nc = bass.Bass("TRN2", target_bir_lowering=False)

    xh = nc.dram_tensor("xh", [TCORE + 2 * HALO, D], F32, kind="ExternalInput").ap()
    w1 = nc.dram_tensor("w1", [D, DFF], F32, kind="ExternalInput").ap()
    w2 = nc.dram_tensor("w2", [DFF, D], F32, kind="ExternalInput").ap()
    wfc = nc.dram_tensor("wfc", [DFF, D], F32, kind="ExternalInput").ap()
    wpr = nc.dram_tensor("wpr", [D, DFF], F32, kind="ExternalInput").ap()
    g1 = nc.dram_tensor("g1", [D], F32, kind="ExternalInput").ap()
    b1ln = nc.dram_tensor("b1ln", [D], F32, kind="ExternalInput").ap()
    g2 = nc.dram_tensor("g2", [D], F32, kind="ExternalInput").ap()
    b2ln = nc.dram_tensor("b2ln", [D], F32, kind="ExternalInput").ap()
    bias1 = nc.dram_tensor("bias1", [DFF], F32, kind="ExternalInput").ap()
    bias2 = nc.dram_tensor("bias2", [D], F32, kind="ExternalInput").ap()
    fastb = nc.dram_tensor("fastb", [16, 16], F32, kind="ExternalInput").ap()
    uu = nc.dram_tensor("uu", [DFF, 16], F32, kind="ExternalInput").ap()
    vv = nc.dram_tensor("vv", [16, D], F32, kind="ExternalInput").ap()
    gate = nc.dram_tensor("gate", [1, 1], F32, kind="ExternalInput").ap()
    y = nc.dram_tensor("y", [TCORE, D], F32, kind="ExternalOutput").ap()

    st_w1 = nc.dram_tensor("st_w1", [NFB, 128, D], BF16, kind="Internal").ap()
    st_w2 = nc.dram_tensor("st_w2", [NFB, 128, D], BF16, kind="Internal").ap()
    st_wfcT = nc.dram_tensor("st_wfcT", [NFB, 128, D], BF16, kind="Internal").ap()
    st_wprT = nc.dram_tensor("st_wprT", [NFB, 128, D], BF16, kind="Internal").ap()

    dbg = {}
    for name in DEBUG_TAPS:
        shape = {
            "vp": [NCHUNK, NTT, 128, H, 65],
            "xnT": [NCHUNK, NDT, 128, WINT],
            "m": [NCHUNK, 16, TC],
            "ln1in": [NCHUNK, NQT, 128, D],
            "x1T": [NCHUNK, NDT, 128, TC],
            "x1tok": [NCHUNK, NQT, 128, D],
            "ln2in": [NCHUNK, NQT, 128, D],
            "rfb": [NCHUNK, NFB, 128, TC],
            "sfc": [NCHUNK, NFB, 128, TC],
            "g2z2": [NCHUNK, NQT, 128, D],
            "stw": [4, NFB, 128, D],
        }[name]
        dt = BF16 if name in ("vp", "xnT", "x1T", "x1tok", "rfb", "sfc", "g2z2",
                              "stw", "m") else F32
        dbg[name] = nc.dram_tensor("dbg_" + name, shape, dt, kind="Internal" if False else "ExternalOutput").ap()

    sch = _Sched()

    import contextlib
    stack = contextlib.ExitStack()
    sb = lambda name, shape, dtype: stack.enter_context(
        nc.sbuf_tensor(name, shape, dtype))

    ident_bf = sb("ident_bf", [128, 128], BF16)
    constcol = sb("constcol", [128, 8], F32)
    ident_f32 = sb("ident_f32", [128, 128], F32)
    maskband = sb("maskband", [128, 640], BF16)
    headmask = sb("headmask", [128, NDT, 16], BF16)
    ones1p = sb("ones1p", [1, 128], BF16)
    g1b1_sb = sb("g1b1_sb", [128, 2 * NDT], F32)
    bias1_sb = sb("bias1_sb", [128, NFB], F32)
    b2row = sb("b2row", [1, D], F32)
    b1rowT = sb("b1rowT", [1, D], F32)
    bt2row = sb("bt2row", [1, D], F32)
    ones1pf = sb("ones1pf", [1, 128], F32)
    fbT_f32 = sb("fbT_f32", [16, 16], F32)
    gate_f = sb("gate_f", [1, 1], F32)

    g1bc = sb("g1bc", [128, D], BF16)
    g2bc = sb("g2bc", [128, D], BF16)
    vT_sb = sb("vT_sb", [128, NDT * 16], BF16)
    fbT_sb = sb("fbT_sb", [16, 16], BF16)
    uTg_sb = sb("uTg_sb", [16, DFF], BF16)
    gate16 = sb("gate16", [16, 1], F32)

    wring = sb("wring", [128, 4, D], BF16)
    NRING = 4

    prep_f32 = sb("prep_f32", [128, DFF], F32)
    g1row = prep_f32[0:1, 0:D]
    g2row = prep_f32[0:1, D:2 * D]
    prep_a = sb("prep_a", [128, DFF], BF16)
    prep_st = sb("prep_st", [128, D], BF16)
    prep_vec = sb("prep_vec", [128, 8], F32)
    prscale = sb("prscale", [128, 4 * NDT], F32)

    xbuf = sb("xbuf", [128, 2, D], F32)
    outbuf = sb("outbuf", [128, 2, D], F32)

    vp = sb("vp", [128, NTT, H, 65], BF16)
    xnT = sb("xnT", [128, NDT, WINT], BF16)
    xnT2 = sb("xnT2", [128, 2, 128], BF16)
    m16 = sb("m16", [16, TC], BF16)
    qaug = sb("qaug", [65, 2, TC], BF16)
    kaug = sb("kaug", [65, 2, WINT], BF16)
    psb = sb("psb", [128, 3, 640], BF16)
    avT = sb("avT", [65, 2, TC], F32)
    rz = sb("rz", [128, NQT], F32)
    ln1in = sb("ln1in", [128, NQT, D], F32)
    t1s = sb("t1s", [128, NQT, D], BF16)
    x1T = sb("x1T", [128, NDT, TC], BF16)
    x1tok = sb("x1tok", [128, NQT, D], BF16)
    pool = sb("pool", [128, NFB, TC], BF16)
    lr_sb = sb("lr_sb", [16, 2, TC], BF16)
    stat = sb("stat", [128, 64], F32)
    sqscr = sb("sqscr", [128, D], BF16)

    psum = stack.enter_context(nc.psum_tensor("pp", [128, 4096], F32))

    bufs = {}

    def B_(*key):
        if key not in bufs:
            bufs[key] = _Buf(str(key))
        return bufs[key]

    def PB(i):
        return B_("psum", i)

    psum_bf = psum.bitcast(BF16)

    def bank(i, lo=0, hi=512, part=128):
        return psum[0:part, 512 * i + lo: 512 * i + hi]

    def bank_bf(i, lo=0, hi=1024, part=128):
        return psum_bf[0:part, 1024 * i + lo: 1024 * i + hi]

    def PQ(i):
        return B_("pf32q", i)

    def E8(i, w=512):
        return prep_f32[:, 512 * i:512 * i + w]

    def EB(i):
        return B_("pf32e", i)

    def dma(eng, out_ap, in_ap, reads, writes, tag, slow=False, q=None,
            barrier=False):
        return sch.op(eng, lambda e, o=out_ap, i=in_ap, s=slow: e.dma_start(
            out=o, in_=i, allow_slow_non_contiguous=s),
            reads=reads, writes=writes, dma=True, tag=tag, q=q, barrier=barrier)

    for nm, ap_ in (("ident_bf", ident_bf), ("ident_f32", ident_f32)):
        sch.op("gp", lambda e, ap_=ap_: e.memset(ap_[:, :], 0.0), [],
               [B_(nm)], tag=nm + "_z")
        sch.op("gp", lambda e, ap_=ap_: e.affine_select(
            out=ap_[:, :], in_=ap_[:, :], compare_op=ALU.not_equal,
            fill=1.0, base=0, pattern=[[-1, 128]], channel_multiplier=1),
            [B_(nm)], [B_(nm)], tag=nm)

    sch.op("gp", lambda e: e.memset(maskband[:, :], 1.0), [], [B_("maskband")],
           tag="mb_z")
    sch.op("gp", lambda e: e.affine_select(
        out=maskband[:, :], in_=maskband[:, :], compare_op=ALU.is_ge,
        fill=0.0, base=0, pattern=[[1, 640]], channel_multiplier=-1),
        [B_("maskband")], [B_("maskband")], tag="mb_1")
    sch.op("gp", lambda e: e.affine_select(
        out=maskband[:, :], in_=maskband[:, :], compare_op=ALU.is_ge,
        fill=0.0, base=512, pattern=[[-1, 640]], channel_multiplier=1),
        [B_("maskband")], [B_("maskband")], tag="mb_2")

    for dt in range(NDT):
        sch.op("gp", lambda e, dt=dt: e.memset(headmask[:, dt, :], 0.0), [],
               [B_("hm", dt)], tag=f"hm_z{dt}")
        sch.op("gp", lambda e, dt=dt: e.memset(
            headmask[0:64, dt, 2 * dt:2 * dt + 1], 1.0),
            [B_("hm", dt)], [B_("hm", dt)], tag=f"hm_a{dt}")
        sch.op("gp", lambda e, dt=dt: e.memset(
            headmask[64:128, dt, 2 * dt + 1:2 * dt + 2], 1.0),
            [B_("hm", dt)], [B_("hm", dt)], tag=f"hm_b{dt}")

    sch.op("gp", lambda e: e.memset(ones1p[:, :], 1.0), [], [B_("ones1p")], tag="ones1p")
    sch.op("gp", lambda e: e.memset(ones1pf[:, :], 1.0), [], [B_("ones1pf")], tag="ones1pf")
    sch.op("gp", lambda e: e.memset(kaug[64:65, 0, :], 1.0), [], [B_("kaug_ones", 0)], tag="kones0")
    sch.op("gp", lambda e: e.memset(kaug[64:65, 1, :], 1.0), [], [B_("kaug_ones", 1)], tag="kones1")
    for tt in range(NTT):
        sch.op("gp", lambda e, tt=tt: e.memset(vp[:, tt, :, 64:65], 1.0), [],
               [B_("vp_ones", tt)], tag=f"vpones{tt}")

    for src_, dst, tag in ((g1, g1b1_sb[:, 0:NDT], "g1"),
                           (b1ln, g1b1_sb[:, NDT:2 * NDT], "b1ln"),
                           (bias1, bias1_sb[:, :], "bias1")):
        dma("sp", dst, src_.rearrange("(a p) -> p a", p=128), [], [B_(tag)],
            tag="ld_" + tag, slow=True, q=("sm",), barrier=True)
    dma("sp", b2row[:, :], bias2[None, :], [], [B_("b2row")],
        tag="ld_b2row", q=("sm",), barrier=True)
    dma("sp", bt2row[:, :], b2ln[None, :], [], [B_("bt2row")],
        tag="ld_bt2row", q=("sm",), barrier=True)
    dma("sp", g2row, g2[None, :], [],
        [B_("g2row")], tag="ld_g2row", q=("sm",), barrier=True)
    dma("sp", g1row, g1[None, :], [], [B_("g1row")],
        tag="ld_g1row", q=("sm",), barrier=True)
    dma("sp", b1rowT[:, :], b1ln[None, :], [], [B_("b1row")],
        tag="ld_b1row", q=("sm",), barrier=True)
    dma("sp", fbT_f32[:, :], fastb.rearrange("a b -> b a"), [], [B_("fbT32")],
        tag="ld_fbT", slow=True, q=("sm",), barrier=True)
    dma("sp", gate_f[:, :], gate, [], [B_("gate_f")],
        tag="ld_gate", q=("sm",), barrier=True)
    dma("sp", prep_f32[0:16, 2048:3072], vv, [], [EB(4), EB(5)],
        tag="ld_v", q=("sm",), barrier=True)
    dma("sp", prep_f32[:, 3072:3584], uu.rearrange("(p a) r -> p (a r)", p=128),
        [], [EB(6)], tag="ld_u", q=("sm",), barrier=True)

    sch.op("dve", lambda e: e.tensor_copy(out=fbT_sb[:, :], in_=fbT_f32[:, :]),
           [B_("fbT32")], [B_("fbT")], tag="fbT_cast")

    for dt in range(NDT):
        sl = dt % 4
        sch.op("pe", lambda e, dt=dt, sl=sl: e.transpose(
            bank(0, 16 * sl, 16 * sl + 16),
            prep_f32[0:16, 2048 + 128 * dt:2048 + 128 * (dt + 1)],
            ident_f32[0:16, 0:16]),
            [EB(4), EB(5), B_("ident_f32")], [PB(0)], tag=f"vT_t{dt}")
        sch.op("dve", lambda e, dt=dt, sl=sl: e.tensor_copy(
            out=vT_sb[:, dt * 16:(dt + 1) * 16], in_=bank(0, 16 * sl, 16 * sl + 16)),
            [PB(0)], [B_("vT", dt)], tag=f"vT_e{dt}")
    for a in range(32):
        sl = a % 4
        sch.op("pe", lambda e, a=a, sl=sl: e.transpose(
            bank(1, 128 * sl, 128 * sl + 128, part=16),
            prep_f32[:, 3072 + 16 * a:3072 + 16 * (a + 1)], ident_f32[:, :]),
            [EB(6), B_("ident_f32")], [PB(1)], tag=f"uT_t{a}")
        sch.op("dve", lambda e, a=a, sl=sl: e.tensor_copy(
            out=uTg_sb.ap().rearrange("r (f a2) -> r f a2", a2=32)[:, :, a],
            in_=bank(1, 128 * sl, 128 * sl + 128, part=16)),
            [PB(1)], [B_("uTg")], tag=f"uT_e{a}")
    sch.op("pe", lambda e: e.matmul(bank(0, 256, 257, part=16),
                                    ones1pf[0:1, 0:16], gate_f[0:1, 0:1],
                                    start=True, stop=True),
           [B_("ones1pf"), B_("gate_f")], [PB(0)], tag="gate_mm")
    sch.op("dve", lambda e: e.tensor_copy(out=gate16[:, :],
                                          in_=bank(0, 256, 257, part=16)),
           [PB(0)], [B_("gate16")], tag="gate_evac")
    sch.op("dve", lambda e: e.tensor_scalar_mul(uTg_sb[:, :], uTg_sb[:, :],
                                                gate16[0:16, 0:1]),
           [B_("uTg"), B_("gate16")], [B_("uTg")], tag="uTg_scale")

    for rowsrc, dstbc, nm2 in ((g1row, g1bc, "g1bc"),):
        for half in range(2):
            sch.op("pe", lambda e, r=rowsrc, h=half: e.matmul(
                bank(2 + h), ones1pf[0:1, :], r[0:1, 512 * h:512 * (h + 1)],
                start=True, stop=True),
                [B_("ones1pf"), B_(nm2[0:2] + "row")], [PB(2 + half)],
                tag=f"{nm2}_mm{half}")
            sch.op("dve", lambda e, d=dstbc, h=half: e.tensor_copy(
                out=d[:, 512 * h:512 * (h + 1)], in_=bank(2 + h)),
                [PB(2 + half)], [B_(nm2)], tag=f"{nm2}_e{half}")
    for half in range(2):
        sch.op("pe", lambda e, h=half: e.matmul(
            bank(2 + h), ones1pf[0:1, :], g2row[0:1, 512 * h:512 * (h + 1)],
            start=True, stop=True),
            [B_("ones1pf"), B_("g2row")], [PB(2 + half)], tag=f"g2bc_mm{half}")
        sch.op("dve", lambda e, h=half: e.tensor_copy(
            out=g2bc[:, 512 * h:512 * (h + 1)], in_=bank(2 + h)),
            [PB(2 + half)], [B_("g2bc")], tag=f"g2bc_evac{half}")

    thunks_w12 = []
    w1r = w1.rearrange("(dt p) f -> p dt f", p=128)

    def _w1_step(fb):
        rb = fb % 2
        rowdep = ([B_("g1row")] if fb == 0 else
                  [B_("g2row")] if fb == 1 else [])
        dma("sp", prep_f32[:, 1024 * rb:1024 * (rb + 1)].rearrange(
            "p (a b) -> p a b", a=NDT),
            w1r[:, :, 128 * fb:128 * (fb + 1)],
            [], [EB(2 * rb), EB(2 * rb + 1)] + rowdep,
            tag=f"w1ld{fb}", q=("wld", rb))
        sch.op("sc", lambda e, rb=rb: e.activation(
            out=wring[:, rb, :], in_=prep_f32[:, 1024 * rb:1024 * (rb + 1)],
            func=AF.Identity),
            [EB(2 * rb), EB(2 * rb + 1)], [B_("wring", rb)], tag=f"w1c{fb}")
        dma("sc", st_w1[fb], wring[:, rb, :], [B_("wring", rb)],
            [B_("st_w1", fb)], tag=f"w1st{fb}", q=("stg1", rb))

    def _w2_step(fb):
        rb = fb % 2
        dma("sp", prep_f32[:, 1024 * (2 + rb):1024 * (3 + rb)],
            w2[128 * fb:128 * (fb + 1), :],
            [], [EB(4 + 2 * rb), EB(5 + 2 * rb)],
            tag=f"w2ld{fb}", q=("wld", 2 + rb))
        sch.op("sc", lambda e, rb=rb: e.activation(
            out=wring[:, 2 + rb, :], in_=prep_f32[:, 1024 * (2 + rb):1024 * (3 + rb)],
            func=AF.Identity),
            [EB(4 + 2 * rb), EB(5 + 2 * rb)], [B_("wring", 2 + rb)],
            tag=f"w2c{fb}")
        dma("sc", st_w2[fb], wring[:, 2 + rb, :], [B_("wring", 2 + rb)],
            [B_("st_w2", fb)], tag=f"w2st{fb}", q=("stg2", rb))

    for fb in range(NFB):
        thunks_w12.append(lambda fb=fb: _w1_step(fb))
        thunks_w12.append(lambda fb=fb: _w2_step(fb))

    GRID = 32768.0
    RBIG = 8388608.0
    _CONSTS = (GRID, 1.0 / GRID, RBIG, -256.0, 2.0 * D, 2.0 * DFF)
    for _i, _v in enumerate(_CONSTS):
        nc.const_aps.register(F32, _v, constcol[:, _i:_i + 1])
        sch.op("gp", lambda e, _i=_i, _v=_v: e.memset(constcol[:, _i:_i + 1], _v),
               [], [B_("constcol")], tag=f"const{_i}")
    Q1 = prep_f32[:, 1024:2048]
    Q2 = prep_f32[:, 2048:3072]
    Q3 = prep_f32[:, 3072:4096]

    def emit_split_sum(wq, wbufs, shi, slo, first, statbuf, base, tag):
        A, HQ, LQ = E8(base + 1), E8(base + 2), E8(base + 3)
        ab, hb, lb = EB(base + 1), EB(base + 2), EB(base + 3)
        sch.op("sc", lambda e: e.activation(out=A, in_=wq, func=AF.Abs),
               wbufs, [ab], tag=tag + "a")
        sch.op("sc", lambda e: e.activation(out=HQ, in_=A, func=AF.Identity,
                                            scale=GRID, bias=RBIG),
               [ab, B_("constcol")], [hb], tag=tag + "h1")
        sch.op("sc", lambda e: e.activation(out=HQ, in_=HQ, func=AF.Identity,
                                            scale=1.0 / GRID, bias=-256.0),
               [hb, B_("constcol")], [hb], tag=tag + "h2")
        sch.op("gp", lambda e: e.tensor_sub(LQ, A, HQ),
               [ab, hb], [lb], tag=tag + "lo")
        if first:
            def _r(e):
                e.reduce_sum(shi, HQ, axis=mybir.AxisListType.X)
                e.drain()
                return e.reduce_sum(slo, LQ, axis=mybir.AxisListType.X)
            sch.op("dve", _r, [hb, lb], [statbuf], tag=tag + "r")
        else:
            sx = 6 if base == 0 else 7

            def _r(e, sx=sx):
                e.reduce_sum(prep_vec[:, sx:sx + 1], HQ,
                             axis=mybir.AxisListType.X)
                e.drain()
                e.tensor_scalar(shi, shi, prep_vec[:, sx:sx + 1], 0.0,
                                ALU.add, ALU.add)
                e.drain()
                e.reduce_sum(prep_vec[:, sx:sx + 1], LQ,
                             axis=mybir.AxisListType.X)
                e.drain()
                return e.tensor_scalar(slo, slo, prep_vec[:, sx:sx + 1], 0.0,
                                       ALU.add, ALU.add)
            sch.op("dve", _r, [hb, lb, statbuf],
                   [statbuf, B_("prep_vec")], tag=tag + "r")

    def emit_tern(wq, wbufs, shi, negshi, slo, scale, dst, dstbufs, statbuf,
                  base, tag, n):
        S1, S2, OT = E8(base + 1), E8(base + 2), E8(base + 3)
        s1b, s2b, ob = EB(base + 1), EB(base + 2), EB(base + 3)
        sch.op("sc", lambda e: e.activation(out=S1, in_=wq, func=AF.Identity,
                                            scale=2.0 * n, bias=negshi),
               wbufs + [statbuf, B_("constcol")], [s1b], tag=tag + "s1")
        sch.op("sc", lambda e: e.activation(out=S2, in_=wq, func=AF.Identity,
                                            scale=2.0 * n, bias=shi),
               wbufs + [statbuf, B_("constcol")], [s2b], tag=tag + "s2")
        sch.op("dve", lambda e: e.tensor_scalar(S1, S1, slo, 0.0,
                                                ALU.subtract, ALU.is_gt),
               [s1b, statbuf], [s1b], tag=tag + "g")
        sch.op("dve", lambda e: e.tensor_scalar(S2, S2, slo, 0.0,
                                                ALU.add, ALU.is_lt),
               [s2b, statbuf], [s2b], tag=tag + "l")

        def _wt(e):
            e.tensor_sub(OT, S1, S2)
            e.drain()
            return e.tensor_scalar(dst, OT, scale, 0.0, ALU.mult, ALU.add)
        sch.op("dve", _wt, [s1b, s2b, statbuf], [ob] + dstbufs, tag=tag + "w")

    thunks_fc = []

    def _fc_step(ft):
        for hf in range(2):
            base = 4 * hf
            dma("sp", E8(base), wfc[128 * ft:128 * (ft + 1),
                                    512 * hf:512 * (hf + 1)],
                [], [EB(base)], tag=f"fc_ld{ft}_{hf}", q=("prep", hf))
            emit_split_sum(E8(base), [EB(base)], prep_vec[:, 0:1],
                           prep_vec[:, 1:2], hf == 0, B_("prep_vec"),
                           base, f"fc{ft}_{hf}")

        def _sc2(e):
            e.tensor_scalar(prep_vec[:, 3:4], prep_vec[:, 0:1],
                            prep_vec[:, 1:2], 0.0, ALU.add, ALU.add)
            e.tensor_scalar(prep_vec[:, 4:5], prep_vec[:, 0:1], -1.0, 0.0,
                            ALU.mult, ALU.add)
            e.drain()
            return e.tensor_scalar(prep_vec[:, 3:4], prep_vec[:, 3:4],
                                   1.0 / D, 1e-5, ALU.mult, ALU.max)

        sch.op("dve", _sc2, [B_("prep_vec")], [B_("prep_vec")],
               tag=f"fc{ft}_sc")
        for hf in range(2):
            base = 4 * hf
            emit_tern(E8(base), [EB(base)], prep_vec[:, 0:1],
                      prep_vec[:, 4:5], prep_vec[:, 1:2], prep_vec[:, 3:4],
                      prep_a[:, 512 * hf:512 * (hf + 1)],
                      [B_("prep_a", hf)], B_("prep_vec"), base,
                      f"fc{ft}t{hf}", D)
        for dt in range(NDT):
            sch.op("pe", lambda e, dt=dt: e.transpose(
                bank_bf(5, 128 * dt, 128 * (dt + 1)),
                prep_a[:, 128 * dt:128 * (dt + 1)], ident_bf[:, :]),
                [B_("prep_a", dt // 4), B_("ident_bf")], [PB(5)],
                tag=f"fcT{ft}_{dt}")
        sch.op("dve", lambda e: e.tensor_copy(
            out=prep_st[:, :], in_=bank_bf(5, 0, 1024)),
            [PB(5)], [B_("prep_st", d2) for d2 in range(NDT)],
            tag=f"fcTe{ft}")
        dma("sc", st_wfcT[ft], prep_st[:, :],
            [B_("prep_st", d2) for d2 in range(NDT)],
            [B_("st_wfcT", ft)], tag=f"fc_st{ft}", q=("stgfc",))

    for ft in range(NFB):
        thunks_fc.append(lambda ft=ft: _fc_step(ft))

    thunks_pr1 = []

    def _pr1_step(dt):
        for qq in range(4):
            for hf in range(2):
                base = 4 * hf
                c0 = 1024 * qq + 512 * hf
                dma("sp", E8(base), wpr[128 * dt:128 * (dt + 1), c0:c0 + 512],
                    [], [EB(base)], tag=f"pr_ld{dt}_{qq}_{hf}",
                    q=("prep", hf))
                emit_split_sum(E8(base), [EB(base)],
                               prscale[:, NDT + dt:NDT + dt + 1],
                               prscale[:, 2 * NDT + dt:2 * NDT + dt + 1],
                               qq == 0 and hf == 0, B_("prscale"),
                               base, f"pr{dt}_{qq}_{hf}")

        def _psc(e, dt=dt):
            e.tensor_scalar(prscale[:, dt:dt + 1],
                            prscale[:, NDT + dt:NDT + dt + 1],
                            prscale[:, 2 * NDT + dt:2 * NDT + dt + 1],
                            0.0, ALU.add, ALU.add)
            e.tensor_scalar(prscale[:, 3 * NDT + dt:3 * NDT + dt + 1],
                            prscale[:, NDT + dt:NDT + dt + 1], -1.0, 0.0,
                            ALU.mult, ALU.add)
            e.drain()
            return e.tensor_scalar(prscale[:, dt:dt + 1], prscale[:, dt:dt + 1],
                                   1.0 / DFF, 1e-5, ALU.mult, ALU.max)

        sch.op("dve", _psc, [B_("prscale")], [B_("prscale")],
               tag=f"pr_sc{dt}")

    for dt in range(NDT):
        thunks_pr1.append(lambda dt=dt: _pr1_step(dt))

    thunks_pr2 = []

    def _pr2_i(dt, i8):
        par = i8 % 2
        base = 4 * par
        c0 = 512 * i8
        dma("sp", E8(base), wpr[128 * dt:128 * (dt + 1), c0:c0 + 512],
            [], [EB(base)], tag=f"prf_ld{dt}_{i8}", q=("prep", par))
        emit_tern(E8(base), [EB(base)],
                  prscale[:, NDT + dt:NDT + dt + 1],
                  prscale[:, 3 * NDT + dt:3 * NDT + dt + 1],
                  prscale[:, 2 * NDT + dt:2 * NDT + dt + 1],
                  prscale[:, dt:dt + 1],
                  prep_a[:, 512 * par:512 * (par + 1)],
                  [B_("prep_a", par)], B_("prscale"), base,
                  f"prf{dt}_{i8}", DFF)
        for k in range(4):
            sch.op("pe", lambda e, k=k, par=par: e.transpose(
                bank_bf(5, 512 * par + 128 * k, 512 * par + 128 * (k + 1)),
                prep_a[:, 512 * par + 128 * k:512 * par + 128 * (k + 1)],
                ident_bf[:, :]),
                [B_("prep_a", par), B_("ident_bf")], [PB(5)],
                tag=f"prfT{dt}_{i8}_{k}")
        sch.op("dve", lambda e, par=par: e.tensor_copy(
            out=prep_st[:, 512 * par:512 * (par + 1)],
            in_=bank_bf(5, 512 * par, 512 * (par + 1))),
            [PB(5)], [B_("prep_st", par)], tag=f"prfTe{dt}_{i8}")
        for k in range(4):
            fc = 4 * i8 + k
            dma("sc", st_wprT[fc][:, 128 * dt:128 * (dt + 1)],
                prep_st[:, 512 * par + 128 * k:512 * par + 128 * (k + 1)],
                [B_("prep_st", par)],
                [B_("st_wprT", fc, dt)], tag=f"prst{dt}_{fc}",
                q=("stgpr", par))

    for dt in range(NDT):
        for i8 in range(8):
            thunks_pr2.append(lambda dt=dt, i8=i8: _pr2_i(dt, i8))

    AXX = mybir.AxisListType.X

    for c in range(NCHUNK):
        w0 = c * TC
        for tt in range(NTT):
            xb = tt % 2
            dma("sp", xbuf[:, xb, :], xh[w0 + 128 * tt: w0 + 128 * (tt + 1), :],
                [], [B_("xbuf", xb)], tag=f"c{c}xld{tt}", q=("x", xb))
            xt = xbuf[:, xb, :]
            sp_ = 8 * tt

            def _stats(e, xt=xt, sp_=sp_):
                e.reduce_sum(stat[:, sp_ + 0:sp_ + 1], xt, axis=AXX)
                e.scalar_tensor_tensor(out=sqscr[:, :], in0=xt, scalar=1.0,
                                       in1=xt, op0=ALU.mult, op1=ALU.mult,
                                       accum_out=stat[:, sp_ + 1:sp_ + 2])
                e.drain()
                e.tensor_scalar(stat[:, sp_ + 2:sp_ + 3], stat[:, sp_ + 0:sp_ + 1],
                                1.0 / D, None, ALU.mult)
                e.drain()
                e.tensor_scalar(stat[:, sp_ + 3:sp_ + 4], stat[:, sp_ + 2:sp_ + 3],
                                stat[:, sp_ + 2:sp_ + 3], EPS,
                                ALU.mult, ALU.subtract)
                e.drain()
                return e.scalar_tensor_tensor(
                    out=stat[:, sp_ + 4:sp_ + 5], in0=stat[:, sp_ + 1:sp_ + 2],
                    scalar=1.0 / D, in1=stat[:, sp_ + 3:sp_ + 4],
                    op0=ALU.mult, op1=ALU.subtract)

            sch.op("dve", _stats, [B_("xbuf", xb)],
                   [B_("stat", tt), B_("sqh", 0), B_("sqh", 1)],
                   tag=f"c{c}stats{tt}")
            sch.op("sc", lambda e, sp_=sp_: e.activation(
                out=stat[:, sp_ + 5:sp_ + 6], in_=stat[:, sp_ + 4:sp_ + 5],
                func=AF.Sqrt),
                [B_("stat", tt)], [B_("stat", tt)], tag=f"c{c}sqrt{tt}")

            def _rstd(e, sp_=sp_):
                e.reciprocal(stat[:, sp_ + 6:sp_ + 7], stat[:, sp_ + 5:sp_ + 6])
                e.drain()
                return e.tensor_scalar(stat[:, sp_ + 7:sp_ + 8],
                                       stat[:, sp_ + 2:sp_ + 3],
                                       stat[:, sp_ + 6:sp_ + 7],
                                       -1.0, ALU.mult, ALU.mult)

            sch.op("dve", _rstd, [B_("stat", tt)], [B_("stat", tt)],
                   tag=f"c{c}rstd{tt}")
            sch.op("sc", lambda e, xt=xt, tt=tt, sp_=sp_: e.activation(
                out=vp[:, tt, :, 0:64],
                in_=xt.rearrange("p (h d) -> p h d", h=H),
                func=AF.Identity, bias=stat[:, sp_ + 7:sp_ + 8],
                scale=stat[:, sp_ + 6:sp_ + 7]),
                [B_("xbuf", xb), B_("stat", tt)], [B_("vp", tt)],
                tag=f"c{c}vp{tt}")
        if c == 0:
            for t_ in thunks_w12:
                t_()
        for tt in range(NTT):
            for g in range(2):
                pb = 4 + g
                for k in range(4):
                    dt = g * 4 + k
                    for hh in (2 * dt, 2 * dt + 1):
                        p0 = (hh % 2) * 64
                        sch.op("pe", lambda e, tt=tt, pb=pb, k=k, hh=hh, p0=p0:
                               e.transpose(
                                   psum_bf[p0:p0 + 64,
                                           1024 * pb + 128 * k:1024 * pb + 128 * (k + 1)],
                                   vp[:, tt, hh, 0:64], ident_bf[:, :]),
                               [B_("vp", tt), B_("ident_bf")], [PB(pb)],
                               tag=f"c{c}xT{tt}_{hh}")
                sch.op("dve", lambda e, tt=tt, g=g, pb=pb: e.tensor_copy(
                    out=xnT[:, 4 * g:4 * (g + 1), 128 * tt:128 * (tt + 1)],
                    in_=bank_bf(pb, 0, 512).rearrange("p (a b) -> p a b", a=4)),
                    [PB(pb)], [B_("xnT", tt, g)], tag=f"c{c}xTe{tt}_{g}")
            if 2 <= tt < 2 + NQT:
                qi = tt - 2
                for dt in range(NDT):
                    sch.op("sc", lambda e, dt=dt, tt=tt:
                           e.activation(out=xnT2[:, dt % 2, :],
                                        in_=xnT[:, dt, 128 * tt:128 * (tt + 1)],
                                        func=AF.Square),
                           [B_("xnT", tt, dt // 4)], [B_("xnT2", dt % 2)],
                           tag=f"c{c}sq{tt}_{dt}")
                    sch.op("pe", lambda e, dt=dt, qi=qi: e.matmul(
                        bank(6, 128 * qi, 128 * (qi + 1), part=16),
                        headmask[:, dt, :], xnT2[:, dt % 2, :],
                        start=(dt == 0), stop=(dt == NDT - 1)),
                        [B_("hm", dt), B_("xnT2", dt % 2)], [PB(6)],
                        tag=f"c{c}mm{tt}_{dt}")
                sch.op("dve", lambda e, qi=qi: e.tensor_scalar_mul(
                    m16[:, 128 * qi:128 * (qi + 1)],
                    bank(6, 128 * qi, 128 * (qi + 1), part=16), -1.0),
                    [PB(6)], [B_("m16", qi)], tag=f"c{c}me{tt}")

        NKB = WINT // 128
        KB_ORDER = [3] + [k for k in range(NKB) if k != 3]

        def emit_aug(h):
            hs = h % 2
            dt_h, r0 = h // 2, (h % 2) * 64
            xnT_keys = [B_("xnT", tt2, dt_h // 4) for tt2 in range(NTT)]
            dma("sc", kaug[0:64, hs, :], xnT[r0:r0 + 64, dt_h, :],
                xnT_keys, [B_("kaug", hs)], tag=f"c{c}h{h}kaug",
                q=("kaug", hs))
            dma("sc", qaug[0:64, hs, :], xnT[r0:r0 + 64, dt_h, HALO:HALO + TC],
                xnT_keys, [B_("qaug", hs)], tag=f"c{c}h{h}qaug",
                q=("qaug", hs))
            dma("sc", qaug[64:65, hs, :], m16[h:h + 1, :],
                [B_("m16", qi2) for qi2 in range(NQT)], [B_("qaugm", hs)],
                tag=f"c{c}h{h}qm", q=("qaugm", hs))

        emit_aug(0)
        for h in range(H):
            hs = h % 2
            avb = 6 + (h % 2)
            ptb = h % 2

            def emit_post(idx, h=h, hs=hs, avb=avb):
                ikb = KB_ORDER[idx]
                qlo = max(ikb * 128 - 256, HALO)
                qhi = min(ikb * 128 + 384, HALO + TC)
                ql, qh = qlo - HALO, qhi - HALO
                w = qh - ql
                sbk = 2 + (idx % 3)
                ps = idx % 3
                sch.op("sc", lambda e, sbk=sbk, w=w, ps=ps: e.activation(
                    out=psb[:, ps, 0:w], in_=bank(sbk, 0, w), func=AF.Exp),
                    [PB(sbk)], [B_("psb", ps)], tag=f"c{c}h{h}e{ikb}")
                j0 = qlo - (ikb * 128 - 256)
                for jr0, jr1 in ((0, 128), (513, 640)):
                    lo = max(jr0, j0)
                    hi = min(jr1, j0 + w)
                    if lo < hi:
                        sch.op("gp", lambda e, ps=ps, lo=lo, hi=hi, j0=j0:
                               e.tensor_mul(psb[:, ps, lo - j0:hi - j0],
                                            psb[:, ps, lo - j0:hi - j0],
                                            maskband[:, lo:hi]),
                               [B_("psb", ps), B_("maskband")], [B_("psb", ps)],
                               tag=f"c{c}h{h}m{ikb}_{lo}")
                sch.op("pe", lambda e, ikb=ikb, h=h, ps=ps, ql=ql, qh=qh,
                       avb=avb, w=w: e.matmul(
                           bank(avb, ql, qh, part=65),
                           vp[:, ikb, h, 0:65], psb[:, ps, 0:w],
                           start=(ikb == KB_ORDER[0]),
                           stop=(ikb == KB_ORDER[-1])),
                       [B_("psb", ps), B_("vp", ikb), B_("vp_ones", ikb)],
                       [PB(avb)], tag=f"c{c}h{h}av{ikb}")

            def emit_smm(idx, h=h, hs=hs):
                ikb = KB_ORDER[idx]
                qlo = max(ikb * 128 - 256, HALO)
                qhi = min(ikb * 128 + 384, HALO + TC)
                ql, qh = qlo - HALO, qhi - HALO
                sbk = 2 + (idx % 3)
                sch.op("pe", lambda e, hs=hs, ikb=ikb, ql=ql, qh=qh, sbk=sbk:
                       e.matmul(bank(sbk, 0, qh - ql),
                                kaug[0:65, hs, 128 * ikb:128 * (ikb + 1)],
                                qaug[0:65, hs, ql:qh], start=True, stop=True),
                       [B_("kaug", hs), B_("kaug_ones", hs), B_("qaug", hs),
                        B_("qaugm", hs)],
                       [PB(sbk)], tag=f"c{c}h{h}s{ikb}")

            emit_smm(0)
            emit_smm(1)
            if h + 1 < H:
                emit_aug(h + 1)
            for idx in range(NKB):
                if idx + 2 < NKB:
                    emit_smm(idx + 2)
                emit_post(idx)
            sch.op("dve", lambda e, hs=hs, avb=avb: e.tensor_copy(
                out=avT[0:65, hs, :], in_=bank(avb, 0, 512, part=65)),
                [PB(avb)], [B_("avT", hs)], tag=f"c{c}h{h}avE")
            for tb in range(NQT):
                sch.op("pe", lambda e, hs=hs, tb=tb, ptb=ptb: e.transpose(
                    bank(ptb, 65 * tb, 65 * (tb + 1)),
                    avT[0:65, hs, 128 * tb:128 * (tb + 1)],
                    ident_f32[0:65, 0:65]),
                    [B_("avT", hs), B_("ident_f32")], [PB(ptb)],
                    tag=f"c{c}h{h}T{tb}")
            sch.op("dve", lambda e, ptb=ptb: e.reciprocal(
                out=rz[:, 0:NQT],
                in_=bank(ptb, 0, 65 * NQT).rearrange(
                    "p (s x) -> p s x", x=65)[:, :, 64]),
                [PB(ptb)], [B_("rz")], tag=f"c{c}h{h}rz")
            for tb in range(NQT):
                sch.op("dve", lambda e, h=h, tb=tb, ptb=ptb:
                       e.scalar_tensor_tensor(
                           out=ln1in[:, tb, 64 * h:64 * (h + 1)],
                           in0=bank(ptb, 65 * tb, 65 * tb + 64),
                           scalar=rz[:, tb:tb + 1],
                           in1=vp[:, 2 + tb, h, 0:64],
                           op0=ALU.mult, op1=ALU.add),
                       [PB(ptb), B_("rz"), B_("vp", 2 + tb),
                        B_("vp_ones", 2 + tb)],
                       [B_("ln1", tb, h)], tag=f"c{c}h{h}j{tb}")

        if c == 0:
            for t_ in thunks_fc:
                t_()
        if "ln1in" in dbg:
            for tb in range(NQT):
                dma("sc", dbg["ln1in"][c, tb], ln1in[:, tb, :],
                    [B_("ln1", tb, h2) for h2 in range(H)],
                    [B_("dbgl1", c, tb)], tag=f"dbgl1{c}_{tb}",
                    q=("dbg", c), barrier=True)

        def ln_stats(x_ap, reads, par, tag):
            sp_ = 8 * par

            def _st(e, x_ap=x_ap, sp_=sp_):
                e.reduce_sum(stat[:, sp_ + 0:sp_ + 1], x_ap, axis=AXX)
                e.scalar_tensor_tensor(out=sqscr[:, :], in0=x_ap, scalar=1.0,
                                       in1=x_ap, op0=ALU.mult, op1=ALU.mult,
                                       accum_out=stat[:, sp_ + 1:sp_ + 2])
                e.drain()
                e.tensor_scalar(stat[:, sp_ + 2:sp_ + 3],
                                stat[:, sp_ + 0:sp_ + 1], 1.0 / D, None,
                                ALU.mult)
                e.drain()
                e.tensor_scalar(stat[:, sp_ + 3:sp_ + 4],
                                stat[:, sp_ + 2:sp_ + 3],
                                stat[:, sp_ + 2:sp_ + 3], EPS,
                                ALU.mult, ALU.subtract)
                e.drain()
                return e.scalar_tensor_tensor(
                    out=stat[:, sp_ + 4:sp_ + 5], in0=stat[:, sp_ + 1:sp_ + 2],
                    scalar=1.0 / D, in1=stat[:, sp_ + 3:sp_ + 4],
                    op0=ALU.mult, op1=ALU.subtract)

            sch.op("dve", _st, reads, [B_("stat", par), B_("sqh", 0), B_("sqh", 1)],
                   tag=tag + "_st")
            sch.op("sc", lambda e, sp_=sp_: e.activation(
                out=stat[:, sp_ + 5:sp_ + 6], in_=stat[:, sp_ + 4:sp_ + 5],
                func=AF.Sqrt),
                [B_("stat", par)], [B_("stat", par)], tag=tag + "_sq")
            sch.op("dve", lambda e, sp_=sp_: e.reciprocal(
                stat[:, sp_ + 6:sp_ + 7], stat[:, sp_ + 5:sp_ + 6]),
                [B_("stat", par)], [B_("stat", par)], tag=tag + "_r")
            return stat[:, sp_ + 2:sp_ + 3], stat[:, sp_ + 6:sp_ + 7]

        for tb in range(NQT):
            l1keys = [B_("ln1", tb, h2) for h2 in range(H)]
            mu_ap, rstd_ap = ln_stats(ln1in[:, tb, :], l1keys, tb,
                                      f"c{c}l1s{tb}")
            sch.op("dve", lambda e, tb=tb, mu_ap=mu_ap, rstd_ap=rstd_ap:
                   e.tensor_scalar(t1s[:, tb, :], ln1in[:, tb, :],
                                   mu_ap, rstd_ap, ALU.subtract, ALU.mult),
                   l1keys + [B_("stat", tb)], [B_("t1s", tb)],
                   tag=f"c{c}t1_{tb}")
        for tb in range(NQT):
            sch.op("gp", lambda e, tb=tb: e.tensor_mul(
                x1tok[:, tb, :], t1s[:, tb, :], g1bc[:, :]),
                [B_("t1s", tb), B_("g1bc")], [B_("x1tok", tb)],
                tag=f"c{c}xg{tb}")
        for dt in range(NDT):
            pb = 4 + dt % 2
            for tb in range(NQT):
                sch.op("pe", lambda e, tb=tb, dt=dt, pb=pb: e.transpose(
                    bank_bf(pb, 128 * tb, 128 * (tb + 1)),
                    t1s[:, tb, 128 * dt:128 * (dt + 1)], ident_bf[:, :]),
                    [B_("t1s", tb), B_("ident_bf")], [PB(pb)],
                    tag=f"c{c}x1T{tb}_{dt}")
            sch.op("dve", lambda e, dt=dt, pb=pb: e.tensor_scalar(
                x1T[:, dt, :], bank_bf(pb, 0, 512),
                g1b1_sb[:, dt:dt + 1],
                g1b1_sb[:, NDT + dt:NDT + dt + 1],
                ALU.mult, ALU.add),
                [PB(pb), B_("g1"), B_("b1ln")],
                [B_("x1T", tb2, dt) for tb2 in range(NQT)],
                tag=f"c{c}x1Te{dt}")

        if c == 0:
            for t_ in thunks_pr1:
                t_()
        for fb in range(NFB):
            rs = fb % 4
            dma("sp", wring[:, rs, :], st_w1[fb],
                [B_("st_w1", fb)], [B_("wring", rs)],
                tag=f"c{c}w1l{fb}", q=("ring", rs))
            hb = 4 + (fb % 2)
            for dt in range(NDT):
                sch.op("pe", lambda e, rs=rs, dt=dt, hb=hb: e.matmul(
                    bank(hb), wring[:, rs, 128 * dt:128 * (dt + 1)],
                    x1T[:, dt, :], start=(dt == 0), stop=(dt == NDT - 1)),
                    [B_("wring", rs)] + [B_("x1T", tb2, dt)
                                         for tb2 in range(NQT)],
                    [PB(hb)], tag=f"c{c}ff1_{fb}_{dt}")
            sch.op("sc", lambda e, fb=fb, hb=hb: e.activation(
                out=pool[:, fb, :], in_=bank(hb), func=AF.Relu,
                bias=bias1_sb[:, fb:fb + 1]),
                [PB(hb), B_("bias1")], [B_("pool", fb)],
                tag=f"c{c}relu{fb}")

        for tb in range(NQT):
            for dg in range(2):
                bi = 2 * tb + dg
                sch.op("pe", lambda e, bi=bi, dg=dg: e.matmul(
                    bank(bi), ones1pf[0:1, :],
                    b2row[0:1, 512 * dg:512 * (dg + 1)],
                    start=True, stop=False),
                    [B_("ones1pf"), B_("b2row")], [PB(bi)],
                    tag=f"c{c}b2_{tb}_{dg}")
                sch.op("pe", lambda e, bi=bi, dg=dg: e.matmul(
                    bank(bi), ones1pf[0:1, :],
                    b1rowT[0:1, 512 * dg:512 * (dg + 1)],
                    start=False, stop=False),
                    [B_("ones1pf"), B_("b1row")], [PB(bi)],
                    tag=f"c{c}bb1_{tb}_{dg}")
        for fb in range(NFB):
            rs = fb % 4
            dma("sp", wring[:, rs, :], st_w2[fb],
                [B_("st_w2", fb)], [B_("wring", rs)],
                tag=f"c{c}w2l{fb}", q=("ring", rs))
            for tb in range(NQT):
                for dg in range(2):
                    bi = 2 * tb + dg
                    sch.op("pe", lambda e, rs=rs, fb=fb, tb=tb, dg=dg, bi=bi:
                           e.matmul(bank(bi),
                                    pool[:, fb, 128 * tb:128 * (tb + 1)],
                                    wring[:, rs, 512 * dg:512 * (dg + 1)],
                                    start=False, stop=(fb == NFB - 1)),
                           [B_("pool", fb), B_("wring", rs)], [PB(bi)],
                           tag=f"c{c}ff2_{fb}_{tb}_{dg}")

        for tb in range(NQT):
            l1keys = [B_("ln1", tb, h2) for h2 in range(H)]
            sch.op("dve", lambda e, tb=tb: e.scalar_tensor_tensor(
                out=ln1in[:, tb, :], in0=bank(2 * tb, 0, 512 * 2),
                scalar=1.0, in1=x1tok[:, tb, :],
                op0=ALU.bypass, op1=ALU.add),
                [PB(2 * tb), PB(2 * tb + 1), B_("x1tok", tb)],
                l1keys, tag=f"c{c}l2j{tb}")
        if c == 0:
            for t_ in thunks_pr2:
                t_()
        for dt in range(NDT):
            sch.op("pe", lambda e, dt=dt: e.matmul(
                bank(4, 0, 512, part=16), vT_sb[:, 16 * dt:16 * (dt + 1)],
                xnT[:, dt, HALO:HALO + TC],
                start=(dt == 0), stop=(dt == NDT - 1)),
                [B_("vT", dt)] + [B_("xnT", tt2, dt // 4)
                                  for tt2 in range(NTT)],
                [PB(4)], tag=f"c{c}t1T_{dt}")
        sch.op("dve", lambda e: e.tensor_copy(out=lr_sb[0:16, 0, :],
                                              in_=bank(4, 0, 512, part=16)),
               [PB(4)], [B_("lr", 0)], tag=f"c{c}t1Te")
        sch.op("pe", lambda e: e.matmul(bank(5, 0, 512, part=16),
                                        fbT_sb[:, :], lr_sb[0:16, 0, :],
                                        start=True, stop=True),
               [B_("fbT"), B_("lr", 0)], [PB(5)], tag=f"c{c}t2T")
        sch.op("dve", lambda e: e.tensor_copy(out=lr_sb[0:16, 1, :],
                                              in_=bank(5, 0, 512, part=16)),
               [PB(5)], [B_("lr", 1)], tag=f"c{c}t2Te")
        for fc in range(NFB):
            rs = fc % 4
            dma("sp", wring[:, rs, :], st_wfcT[fc],
                [B_("st_wfcT", fc)], [B_("wring", rs)],
                tag=f"c{c}wfl{fc}", q=("ring", rs))
            hb = 4 + (fc % 2)
            for dt in range(NDT):
                sch.op("pe", lambda e, rs=rs, dt=dt, hb=hb: e.matmul(
                    bank(hb), wring[:, rs, 128 * dt:128 * (dt + 1)],
                    xnT[:, dt, HALO:HALO + TC],
                    start=(dt == 0), stop=False),
                    [B_("wring", rs)] + [B_("xnT", tt2, dt // 4)
                                         for tt2 in range(NTT)],
                    [PB(hb)], tag=f"c{c}fc{fc}_{dt}")
            sch.op("pe", lambda e, fc=fc, hb=hb: e.matmul(
                bank(hb), uTg_sb[0:16, 128 * fc:128 * (fc + 1)],
                lr_sb[0:16, 1, :], start=False, stop=True),
                [B_("uTg"), B_("lr", 1)], [PB(hb)], tag=f"c{c}u{fc}")
            sh = fc % 2
            sch.op("sc", lambda e, hb=hb, sh=sh: e.activation(
                out=sqscr[:, 512 * sh:512 * (sh + 1)], in_=bank(hb),
                func=AF.Relu),
                [PB(hb)], [B_("sqh", sh)], tag=f"c{c}rl2_{fc}")
            sch.op("dve", lambda e, fc=fc, hb=hb, sh=sh: e.tensor_mul(
                pool[:, fc, :], sqscr[:, 512 * sh:512 * (sh + 1)], bank(hb)),
                [PB(hb), B_("sqh", sh)], [B_("pool", fc)], tag=f"c{c}sq2_{fc}")

        for tb in range(NQT):
            l1keys = [B_("ln1", tb, h2) for h2 in range(H)]
            mu2, rstd2 = ln_stats(ln1in[:, tb, :], l1keys, 4 + tb,
                                  f"c{c}l2s{tb}")
            sch.op("dve", lambda e, tb=tb, mu2=mu2, rstd2=rstd2:
                   e.tensor_scalar(sqscr[:, :], ln1in[:, tb, :],
                                   mu2, rstd2, ALU.subtract, ALU.mult),
                   l1keys + [B_("stat", 4 + tb)], [B_("sqh", 0), B_("sqh", 1)],
                   tag=f"c{c}z2_{tb}")
            sch.op("gp", lambda e, tb=tb: e.tensor_mul(
                t1s[:, tb, :], sqscr[:, :], g2bc[:, :]),
                [B_("sqh", 0), B_("sqh", 1), B_("g2bc")], [B_("t1s", tb)],
                tag=f"c{c}g2z{tb}")

        for tb in range(NQT):
            for dg in range(2):
                bi = 2 * tb + dg
                sch.op("pe", lambda e, bi=bi, dg=dg: e.matmul(
                    bank(bi), ones1pf[0:1, :],
                    bt2row[0:1, 512 * dg:512 * (dg + 1)],
                    start=True, stop=False),
                    [B_("ones1pf"), B_("bt2row")], [PB(bi)],
                    tag=f"c{c}bt2_{tb}_{dg}")
        for fc in range(NFB):
            rs = fc % 4
            dma("sp", wring[:, rs, :], st_wprT[fc],
                [B_("st_wprT", fc, dt2) for dt2 in range(NDT)],
                [B_("wring", rs)], tag=f"c{c}wpl{fc}", q=("ring", rs))
            for tb in range(NQT):
                for dg in range(2):
                    bi = 2 * tb + dg
                    sch.op("pe", lambda e, rs=rs, fc=fc, tb=tb, dg=dg, bi=bi:
                           e.matmul(bank(bi),
                                    pool[:, fc, 128 * tb:128 * (tb + 1)],
                                    wring[:, rs, 512 * dg:512 * (dg + 1)],
                                    start=False, stop=(fc == NFB - 1)),
                           [B_("pool", fc), B_("wring", rs)], [PB(bi)],
                           tag=f"c{c}pj{fc}_{tb}_{dg}")

        for tb in range(NQT):
            xb2 = tb % 2
            dma("sp", xbuf[:, xb2, :],
                xh[HALO + c * TC + 128 * tb: HALO + c * TC + 128 * (tb + 1), :],
                [], [B_("xbuf", xb2)], tag=f"c{c}xr{tb}", q=("x", xb2))
            l1keys = [B_("ln1", tb, h2) for h2 in range(H)]
            sch.op("dve", lambda e, tb=tb, xb2=xb2: e.scalar_tensor_tensor(
                out=ln1in[:, tb, :], in0=bank(2 * tb, 0, 1024),
                scalar=1.0, in1=xbuf[:, xb2, :],
                op0=ALU.bypass, op1=ALU.add),
                [PB(2 * tb), PB(2 * tb + 1), B_("xbuf", xb2)],
                l1keys, tag=f"c{c}a1_{tb}")
            ob = tb % 2
            sch.op("dve", lambda e, tb=tb, ob=ob: e.tensor_add(
                outbuf[:, ob, :], ln1in[:, tb, :], t1s[:, tb, :]),
                l1keys + [B_("t1s", tb)], [B_("outb", ob)],
                tag=f"c{c}out{tb}")
            dma("sc", y[c * TC + 128 * tb: c * TC + 128 * (tb + 1), :],
                outbuf[:, ob, :], [B_("outb", ob)], [B_("y", c, tb)],
                tag=f"c{c}yst{tb}", q=("out", ob))

        if "vp" in dbg:
            for tt in range(NTT):
                dma("sc", dbg["vp"][c, tt], vp[:, tt, :, :], [B_("vp", tt), B_("vp_ones", tt)],
                    [B_("dbgvp", c, tt)], tag=f"dbgvp{c}_{tt}", q=("dbg", c), barrier=True)
        if "xnT" in dbg:
            for dt in range(NDT):
                dma("sc", dbg["xnT"][c, dt], xnT[:, dt, :],
                    [B_("xnT", tt2, dt // 4) for tt2 in range(NTT)],
                    [B_("dbgxnT", c, dt)], tag=f"dbgxnT{c}_{dt}", q=("dbg", c), barrier=True)
        if "m" in dbg:
            dma("sc", dbg["m"][c], m16[:, :], [B_("m16", qi) for qi in range(NQT)],
                [B_("dbgm", c)], tag=f"dbgm{c}", q=("dbg", c), barrier=True)

    import contextlib as _ctx
    with _ctx.ExitStack() as semstack:
        sems = {e: (semstack.enter_context(nc.semaphore(f"s_{e}"))
                    if e != "sp" else None) for e in ENGS}
        qkeys = sch.qkeys()
        qsems = {k: semstack.enter_context(
            nc.semaphore("q_" + "_".join(str(p) for p in k)))
            for k in qkeys}
        run = sch.emit(nc, sems, qsems)
        qeng = {}
        for o in sch.ops:
            if o.dma:
                qeng[o.q] = o.eng
        qfin = {e: [(qsems[k], sch.final_q[k]) for k in qkeys
                    if qeng[k] == e] for e in ENGS}
        with nc.Block() as blk:

            @blk.tensor
            def _(e):
                run("pe", e)

            @blk.vector
            def _(e):
                run("dve", e)

            @blk.scalar
            def _(e):
                run("sc", e)
                for s, v in qfin["sc"]:
                    e.wait_ge(s, v)

            @blk.sync
            def _(e):
                run("sp", e)
                for s, v in qfin["sp"]:
                    e.wait_ge(s, v)

            @blk.gpsimd
            def _(e):
                run("gp", e)
                for s, v in qfin["gp"]:
                    e.wait_ge(s, v)

    stack.close()
    return nc


_NC_CACHE = {}


def _get_nc():
    key = tuple(DEBUG_TAPS)
    if key not in _NC_CACHE:
        _NC_CACHE[key] = build_nc()
    return _NC_CACHE[key]


def kernel(L0_attn_src, L0_attn_ln1_g, L0_attn_ln1_b, L0_attn_ln2_g,
           L0_attn_ln2_b, L0_attn_W1, L0_attn_b1, L0_attn_W2, L0_attn_b2,
           L0_ffn_fast_b, L0_ffn_w_fc, L0_ffn_w_proj, L0_ffn_u, L0_ffn_v,
           L0_ffn_gate):
    x = np.asarray(L0_attn_src, dtype=np.float32)
    nc = _get_nc()
    common = {
        "w1": np.ascontiguousarray(L0_attn_W1, dtype=np.float32),
        "w2": np.ascontiguousarray(L0_attn_W2, dtype=np.float32),
        "wfc": np.ascontiguousarray(L0_ffn_w_fc, dtype=np.float32),
        "wpr": np.ascontiguousarray(L0_ffn_w_proj, dtype=np.float32),
        "g1": np.ascontiguousarray(L0_attn_ln1_g, dtype=np.float32),
        "b1ln": np.ascontiguousarray(L0_attn_ln1_b, dtype=np.float32),
        "g2": np.ascontiguousarray(L0_attn_ln2_g, dtype=np.float32),
        "b2ln": np.ascontiguousarray(L0_attn_ln2_b, dtype=np.float32),
        "bias1": np.ascontiguousarray(L0_attn_b1, dtype=np.float32),
        "bias2": np.ascontiguousarray(L0_attn_b2, dtype=np.float32),
        "fastb": np.ascontiguousarray(L0_ffn_fast_b, dtype=np.float32),
        "uu": np.ascontiguousarray(L0_ffn_u, dtype=np.float32),
        "vv": np.ascontiguousarray(L0_ffn_v, dtype=np.float32),
        "gate": np.asarray(L0_ffn_gate, dtype=np.float32).reshape(1, 1),
    }
    in_maps = []
    for core in range(NCORES):
        b, half = core // 2, core % 2
        lo = half * TCORE - HALO
        hi = half * TCORE + TCORE + HALO
        xhc = np.zeros((TCORE + 2 * HALO, D), dtype=np.float32)
        s0, s1 = max(lo, 0), min(hi, S)
        xhc[s0 - lo: s1 - lo] = x[b, s0:s1]
        m = dict(common)
        m["xh"] = xhc
        in_maps.append(m)
    res = run_bass_kernel_spmd(nc, in_maps, core_ids=list(range(NCORES)))
    out = np.empty((B, S, D), dtype=np.float32)
    for core in range(NCORES):
        b, half = core // 2, core % 2
        out[b, half * TCORE:(half + 1) * TCORE] = res.results[core]["y"]
    globals()["_LAST_RES"] = res
    return out


# version 26
# speedup vs baseline: 1.0331x; 1.0110x over previous
import numpy as np
import concourse.bass as bass
import concourse.mybir as mybir
from concourse.bass_utils import run_bass_kernel_spmd

F32 = mybir.dt.float32
BF16 = mybir.dt.bfloat16
AF = mybir.ActivationFunctionType
ALU = mybir.AluOpType

B, S, D = 4, 8192, 1024
H, DH, WIN_R = 16, 64, 256
DFF = 4096
EPS = 1e-5
NCORES = 8
TCORE = S // 2
HALO = 256
TC = 512
NCHUNK = TCORE // TC
WINT = TC + 2 * HALO
NTT = WINT // 128
NQT = TC // 128
NFB = DFF // 128
NDT = D // 128

DEBUG_TAPS = []
P1_STAGE = 99

ENGS = ("pe", "dve", "sc", "sp", "gp")


class _Buf:
    __slots__ = ("name", "writer", "readers")

    def __init__(self, name):
        self.name = name
        self.writer = None
        self.readers = []


class _Op:
    __slots__ = ("i", "eng", "fn", "dma", "deps", "inc", "marker", "tag",
                 "q", "barrier", "drain")

    def __init__(self, i, eng, fn, dma, tag, q=None, barrier=False):
        self.i, self.eng, self.fn, self.dma, self.tag = i, eng, fn, dma, tag
        self.q, self.barrier = q, barrier
        self.deps = []
        self.inc = False
        self.marker = None
        self.drain = False


class _Sched:

    def __init__(self):
        self.ops = []

    def op(self, eng, fn, reads=(), writes=(), dma=False, tag="", q=None,
           barrier=False):
        if dma:
            assert q is not None, f"dma op {tag} needs a qsem key"
        o = _Op(len(self.ops), eng, fn, dma, tag, q=q, barrier=barrier)
        deps = set()
        def _edge(w):
            if w.eng != eng or w.dma or dma or eng == "gp":
                deps.add(w.i)
            elif eng != "pe":
                o.drain = True
        for b in reads:
            assert b.writer is not None, f"read of uninitialized buf {b.name} by {tag}"
            _edge(b.writer)
        for b in writes:
            w = b.writer
            if w is not None:
                _edge(w)
            for r in b.readers:
                if r.i in deps:
                    continue
                _edge(r)
        o.deps = sorted(deps)
        for b in reads:
            b.readers.append(o)
        for b in writes:
            b.writer = o
            b.readers = []
        self.ops.append(o)
        return o

    def emit(self, nc, sems, qsems):
        for o in self.ops:
            for d in o.deps:
                dep = self.ops[d]
                if not dep.dma:
                    dep.inc = True
        inc_cnt = {e: 0 for e in ENGS}
        q_cnt = {}
        q_tot = {}
        for o in self.ops:
            if o.dma:
                q_tot[o.q] = q_tot.get(o.q, 0) + 16
        for o in self.ops:
            if o.dma:
                q_cnt[o.q] = q_cnt.get(o.q, 0) + 16
                val = q_tot[o.q] if o.barrier else q_cnt[o.q]
                o.marker = (qsems[o.q], val)
            elif o.inc:
                inc_cnt[o.eng] += 1
                o.marker = (sems[o.eng], inc_cnt[o.eng])
        self.final_q = dict(q_tot)
        self.final_inc = dict(inc_cnt)
        streams = {e: [o for o in self.ops if o.eng == e] for e in ENGS}

        def run_stream(eng_name, e):
            hw = {}
            dirty = [False]
            for o in streams[eng_name]:
                for d in o.deps:
                    m = self.ops[d].marker
                    assert m is not None, (
                        f"op {o.i}({o.tag}) waits on {d}"
                        f"({self.ops[d].tag}) which has no marker")
                    sem, val = m
                    key = id(sem)
                    if hw.get(key, 0) < val:
                        e.wait_ge(sem, val)
                        hw[key] = val
                if o.drain and dirty[0]:
                    e.drain()
                    dirty[0] = False
                inst = o.fn(e)
                dirty[0] = True
                if o.dma:
                    inst.then_inc(qsems[o.q], 16)
                elif o.inc:
                    inst.then_inc(sems[o.eng], 1)
        return run_stream

    def qkeys(self):
        return sorted({o.q for o in self.ops if o.dma}, key=str)


def _ceil(a, b):
    return -(-a // b)


def build_nc():
    nc = bass.Bass("TRN2", target_bir_lowering=False)

    xh = nc.dram_tensor("xh", [TCORE + 2 * HALO, D], F32, kind="ExternalInput").ap()
    w1 = nc.dram_tensor("w1", [D, DFF], F32, kind="ExternalInput").ap()
    w2 = nc.dram_tensor("w2", [DFF, D], F32, kind="ExternalInput").ap()
    wfc = nc.dram_tensor("wfc", [DFF, D], F32, kind="ExternalInput").ap()
    wpr = nc.dram_tensor("wpr", [D, DFF], F32, kind="ExternalInput").ap()
    g1 = nc.dram_tensor("g1", [D], F32, kind="ExternalInput").ap()
    b1ln = nc.dram_tensor("b1ln", [D], F32, kind="ExternalInput").ap()
    g2 = nc.dram_tensor("g2", [D], F32, kind="ExternalInput").ap()
    b2ln = nc.dram_tensor("b2ln", [D], F32, kind="ExternalInput").ap()
    bias1 = nc.dram_tensor("bias1", [DFF], F32, kind="ExternalInput").ap()
    bias2 = nc.dram_tensor("bias2", [D], F32, kind="ExternalInput").ap()
    fastb = nc.dram_tensor("fastb", [16, 16], F32, kind="ExternalInput").ap()
    uu = nc.dram_tensor("uu", [DFF, 16], F32, kind="ExternalInput").ap()
    vv = nc.dram_tensor("vv", [16, D], F32, kind="ExternalInput").ap()
    gate = nc.dram_tensor("gate", [1, 1], F32, kind="ExternalInput").ap()
    y = nc.dram_tensor("y", [TCORE, D], F32, kind="ExternalOutput").ap()

    st_w1 = nc.dram_tensor("st_w1", [NFB, 128, D], BF16, kind="Internal").ap()
    st_w2 = nc.dram_tensor("st_w2", [NFB, 128, D], BF16, kind="Internal").ap()
    st_wfcT = nc.dram_tensor("st_wfcT", [NFB, 128, D], BF16, kind="Internal").ap()
    st_wprT = nc.dram_tensor("st_wprT", [NFB, 128, D], BF16, kind="Internal").ap()

    dbg = {}
    for name in DEBUG_TAPS:
        shape = {
            "vp": [NCHUNK, NTT, 128, H, 65],
            "xnT": [NCHUNK, NDT, 128, WINT],
            "m": [NCHUNK, 16, TC],
            "ln1in": [NCHUNK, NQT, 128, D],
            "x1T": [NCHUNK, NDT, 128, TC],
            "x1tok": [NCHUNK, NQT, 128, D],
            "ln2in": [NCHUNK, NQT, 128, D],
            "rfb": [NCHUNK, NFB, 128, TC],
            "sfc": [NCHUNK, NFB, 128, TC],
            "g2z2": [NCHUNK, NQT, 128, D],
            "stw": [4, NFB, 128, D],
        }[name]
        dt = BF16 if name in ("vp", "xnT", "x1T", "x1tok", "rfb", "sfc", "g2z2",
                              "stw", "m") else F32
        dbg[name] = nc.dram_tensor("dbg_" + name, shape, dt, kind="Internal" if False else "ExternalOutput").ap()

    sch = _Sched()

    import contextlib
    stack = contextlib.ExitStack()
    sb = lambda name, shape, dtype: stack.enter_context(
        nc.sbuf_tensor(name, shape, dtype))

    ident_bf = sb("ident_bf", [128, 128], BF16)
    constcol = sb("constcol", [128, 8], F32)
    ident_f32 = sb("ident_f32", [128, 128], F32)
    maskband = sb("maskband", [128, 640], BF16)
    headmask = sb("headmask", [128, NDT, 16], BF16)
    ones1p = sb("ones1p", [1, 128], BF16)
    g1b1_sb = sb("g1b1_sb", [128, 2 * NDT], F32)
    bias1_sb = sb("bias1_sb", [128, NFB], F32)
    b2row = sb("b2row", [1, D], F32)
    b1rowT = sb("b1rowT", [1, D], F32)
    bt2row = sb("bt2row", [1, D], F32)
    ones1pf = sb("ones1pf", [1, 128], F32)
    fbT_f32 = sb("fbT_f32", [16, 16], F32)
    gate_f = sb("gate_f", [1, 1], F32)

    g1bc = sb("g1bc", [128, D], BF16)
    g2bc = sb("g2bc", [128, D], BF16)
    vT_sb = sb("vT_sb", [128, NDT * 16], BF16)
    fbT_sb = sb("fbT_sb", [16, 16], BF16)
    uTg_sb = sb("uTg_sb", [16, DFF], BF16)
    gate16 = sb("gate16", [16, 1], F32)

    wring = sb("wring", [128, 4, D], BF16)
    NRING = 4

    prep_f32 = sb("prep_f32", [128, DFF], F32)
    g1row = prep_f32[0:1, 0:D]
    g2row = prep_f32[0:1, D:2 * D]
    prep_a = sb("prep_a", [128, DFF], BF16)
    prep_st = sb("prep_st", [128, D], BF16)
    prep_vec = sb("prep_vec", [128, 8], F32)
    prscale = sb("prscale", [128, 4 * NDT], F32)

    xbuf = sb("xbuf", [128, 2, D], F32)
    outbuf = sb("outbuf", [128, 2, D], F32)

    vp = sb("vp", [128, NTT, H, 65], BF16)
    xnT = sb("xnT", [128, NDT, WINT], BF16)
    xnT2 = sb("xnT2", [128, 2, 128], BF16)
    m16 = sb("m16", [16, TC], BF16)
    qaug = sb("qaug", [65, 3, TC], BF16)
    kaug = sb("kaug", [65, 3, WINT], BF16)
    psb = sb("psb", [128, 3, 640], BF16)
    avT = sb("avT", [65, 2, TC], F32)
    rz = sb("rz", [128, NQT], F32)
    ln1in = sb("ln1in", [128, NQT, D], F32)
    t1s = sb("t1s", [128, NQT, D], BF16)
    x1T = sb("x1T", [128, NDT, TC], BF16)
    x1tok = sb("x1tok", [128, NQT, D], BF16)
    pool = sb("pool", [128, NFB, TC], BF16)
    lr_sb = sb("lr_sb", [16, 2, TC], BF16)
    stat = sb("stat", [128, 64], F32)
    sqscr = sb("sqscr", [128, D], BF16)

    psum = stack.enter_context(nc.psum_tensor("pp", [128, 4096], F32))

    bufs = {}

    def B_(*key):
        if key not in bufs:
            bufs[key] = _Buf(str(key))
        return bufs[key]

    def PB(i):
        return B_("psum", i)

    psum_bf = psum.bitcast(BF16)

    def bank(i, lo=0, hi=512, part=128):
        return psum[0:part, 512 * i + lo: 512 * i + hi]

    def bank_bf(i, lo=0, hi=1024, part=128):
        return psum_bf[0:part, 1024 * i + lo: 1024 * i + hi]

    def PQ(i):
        return B_("pf32q", i)

    def E8(i, w=512):
        return prep_f32[:, 512 * i:512 * i + w]

    def EB(i):
        return B_("pf32e", i)

    def dma(eng, out_ap, in_ap, reads, writes, tag, slow=False, q=None,
            barrier=False):
        return sch.op(eng, lambda e, o=out_ap, i=in_ap, s=slow: e.dma_start(
            out=o, in_=i, allow_slow_non_contiguous=s),
            reads=reads, writes=writes, dma=True, tag=tag, q=q, barrier=barrier)

    for nm, ap_ in (("ident_bf", ident_bf), ("ident_f32", ident_f32)):
        sch.op("gp", lambda e, ap_=ap_: e.memset(ap_[:, :], 0.0), [],
               [B_(nm)], tag=nm + "_z")
        sch.op("gp", lambda e, ap_=ap_: e.affine_select(
            out=ap_[:, :], in_=ap_[:, :], compare_op=ALU.not_equal,
            fill=1.0, base=0, pattern=[[-1, 128]], channel_multiplier=1),
            [B_(nm)], [B_(nm)], tag=nm)

    sch.op("gp", lambda e: e.memset(maskband[:, :], 1.0), [], [B_("maskband")],
           tag="mb_z")
    sch.op("gp", lambda e: e.affine_select(
        out=maskband[:, :], in_=maskband[:, :], compare_op=ALU.is_ge,
        fill=0.0, base=0, pattern=[[1, 640]], channel_multiplier=-1),
        [B_("maskband")], [B_("maskband")], tag="mb_1")
    sch.op("gp", lambda e: e.affine_select(
        out=maskband[:, :], in_=maskband[:, :], compare_op=ALU.is_ge,
        fill=0.0, base=512, pattern=[[-1, 640]], channel_multiplier=1),
        [B_("maskband")], [B_("maskband")], tag="mb_2")

    for dt in range(NDT):
        sch.op("gp", lambda e, dt=dt: e.memset(headmask[:, dt, :], 0.0), [],
               [B_("hm", dt)], tag=f"hm_z{dt}")
        sch.op("gp", lambda e, dt=dt: e.memset(
            headmask[0:64, dt, 2 * dt:2 * dt + 1], 1.0),
            [B_("hm", dt)], [B_("hm", dt)], tag=f"hm_a{dt}")
        sch.op("gp", lambda e, dt=dt: e.memset(
            headmask[64:128, dt, 2 * dt + 1:2 * dt + 2], 1.0),
            [B_("hm", dt)], [B_("hm", dt)], tag=f"hm_b{dt}")

    sch.op("gp", lambda e: e.memset(ones1p[:, :], 1.0), [], [B_("ones1p")], tag="ones1p")
    sch.op("gp", lambda e: e.memset(ones1pf[:, :], 1.0), [], [B_("ones1pf")], tag="ones1pf")
    for _ka in range(3):
        sch.op("gp", lambda e, _ka=_ka: e.memset(kaug[64:65, _ka, :], 1.0), [],
               [B_("kaug_ones", _ka)], tag=f"kones{_ka}")
    for tt in range(NTT):
        sch.op("gp", lambda e, tt=tt: e.memset(vp[:, tt, :, 64:65], 1.0), [],
               [B_("vp_ones", tt)], tag=f"vpones{tt}")

    for src_, dst, tag in ((g1, g1b1_sb[:, 0:NDT], "g1"),
                           (b1ln, g1b1_sb[:, NDT:2 * NDT], "b1ln"),
                           (bias1, bias1_sb[:, :], "bias1")):
        dma("sp", dst, src_.rearrange("(a p) -> p a", p=128), [], [B_(tag)],
            tag="ld_" + tag, slow=True, q=("sm",), barrier=True)
    dma("sp", b2row[:, :], bias2[None, :], [], [B_("b2row")],
        tag="ld_b2row", q=("sm",), barrier=True)
    dma("sp", bt2row[:, :], b2ln[None, :], [], [B_("bt2row")],
        tag="ld_bt2row", q=("sm",), barrier=True)
    dma("sp", g2row, g2[None, :], [],
        [B_("g2row")], tag="ld_g2row", q=("sm",), barrier=True)
    dma("sp", g1row, g1[None, :], [], [B_("g1row")],
        tag="ld_g1row", q=("sm",), barrier=True)
    dma("sp", b1rowT[:, :], b1ln[None, :], [], [B_("b1row")],
        tag="ld_b1row", q=("sm",), barrier=True)
    dma("sp", fbT_f32[:, :], fastb.rearrange("a b -> b a"), [], [B_("fbT32")],
        tag="ld_fbT", slow=True, q=("sm",), barrier=True)
    dma("sp", gate_f[:, :], gate, [], [B_("gate_f")],
        tag="ld_gate", q=("sm",), barrier=True)
    dma("sp", prep_f32[0:16, 2048:3072], vv, [], [EB(4), EB(5)],
        tag="ld_v", q=("sm",), barrier=True)
    dma("sp", prep_f32[:, 3072:3584], uu.rearrange("(p a) r -> p (a r)", p=128),
        [], [EB(6)], tag="ld_u", q=("sm",), barrier=True)

    sch.op("dve", lambda e: e.tensor_copy(out=fbT_sb[:, :], in_=fbT_f32[:, :]),
           [B_("fbT32")], [B_("fbT")], tag="fbT_cast")

    for dt in range(NDT):
        sl = dt % 4
        sch.op("pe", lambda e, dt=dt, sl=sl: e.transpose(
            bank(0, 16 * sl, 16 * sl + 16),
            prep_f32[0:16, 2048 + 128 * dt:2048 + 128 * (dt + 1)],
            ident_f32[0:16, 0:16]),
            [EB(4), EB(5), B_("ident_f32")], [PB(0)], tag=f"vT_t{dt}")
        sch.op("dve", lambda e, dt=dt, sl=sl: e.tensor_copy(
            out=vT_sb[:, dt * 16:(dt + 1) * 16], in_=bank(0, 16 * sl, 16 * sl + 16)),
            [PB(0)], [B_("vT", dt)], tag=f"vT_e{dt}")
    for a in range(32):
        sl = a % 4
        sch.op("pe", lambda e, a=a, sl=sl: e.transpose(
            bank(1, 128 * sl, 128 * sl + 128, part=16),
            prep_f32[:, 3072 + 16 * a:3072 + 16 * (a + 1)], ident_f32[:, :]),
            [EB(6), B_("ident_f32")], [PB(1)], tag=f"uT_t{a}")
        sch.op("dve", lambda e, a=a, sl=sl: e.tensor_copy(
            out=uTg_sb.ap().rearrange("r (f a2) -> r f a2", a2=32)[:, :, a],
            in_=bank(1, 128 * sl, 128 * sl + 128, part=16)),
            [PB(1)], [B_("uTg")], tag=f"uT_e{a}")
    sch.op("pe", lambda e: e.matmul(bank(0, 256, 257, part=16),
                                    ones1pf[0:1, 0:16], gate_f[0:1, 0:1],
                                    start=True, stop=True),
           [B_("ones1pf"), B_("gate_f")], [PB(0)], tag="gate_mm")
    sch.op("dve", lambda e: e.tensor_copy(out=gate16[:, :],
                                          in_=bank(0, 256, 257, part=16)),
           [PB(0)], [B_("gate16")], tag="gate_evac")
    sch.op("dve", lambda e: e.tensor_scalar_mul(uTg_sb[:, :], uTg_sb[:, :],
                                                gate16[0:16, 0:1]),
           [B_("uTg"), B_("gate16")], [B_("uTg")], tag="uTg_scale")

    for rowsrc, dstbc, nm2 in ((g1row, g1bc, "g1bc"),):
        for half in range(2):
            sch.op("pe", lambda e, r=rowsrc, h=half: e.matmul(
                bank(2 + h), ones1pf[0:1, :], r[0:1, 512 * h:512 * (h + 1)],
                start=True, stop=True),
                [B_("ones1pf"), B_(nm2[0:2] + "row")], [PB(2 + half)],
                tag=f"{nm2}_mm{half}")
            sch.op("dve", lambda e, d=dstbc, h=half: e.tensor_copy(
                out=d[:, 512 * h:512 * (h + 1)], in_=bank(2 + h)),
                [PB(2 + half)], [B_(nm2)], tag=f"{nm2}_e{half}")
    for half in range(2):
        sch.op("pe", lambda e, h=half: e.matmul(
            bank(2 + h), ones1pf[0:1, :], g2row[0:1, 512 * h:512 * (h + 1)],
            start=True, stop=True),
            [B_("ones1pf"), B_("g2row")], [PB(2 + half)], tag=f"g2bc_mm{half}")
        sch.op("dve", lambda e, h=half: e.tensor_copy(
            out=g2bc[:, 512 * h:512 * (h + 1)], in_=bank(2 + h)),
            [PB(2 + half)], [B_("g2bc")], tag=f"g2bc_evac{half}")

    thunks_w12 = []
    w1r = w1.rearrange("(dt p) f -> p dt f", p=128)

    def _w1_step(fb):
        rb = fb % 2
        rowdep = ([B_("g1row")] if fb == 0 else
                  [B_("g2row")] if fb == 1 else [])
        dma("sp", prep_f32[:, 1024 * rb:1024 * (rb + 1)].rearrange(
            "p (a b) -> p a b", a=NDT),
            w1r[:, :, 128 * fb:128 * (fb + 1)],
            [], [EB(2 * rb), EB(2 * rb + 1)] + rowdep,
            tag=f"w1ld{fb}", q=("wld", rb))
        sch.op("sc", lambda e, rb=rb: e.activation(
            out=wring[:, rb, :], in_=prep_f32[:, 1024 * rb:1024 * (rb + 1)],
            func=AF.Identity),
            [EB(2 * rb), EB(2 * rb + 1)], [B_("wring", rb)], tag=f"w1c{fb}")
        dma("sc", st_w1[fb], wring[:, rb, :], [B_("wring", rb)],
            [B_("st_w1", fb)], tag=f"w1st{fb}", q=("stg1", rb))

    def _w2_step(fb):
        rb = fb % 2
        dma("sp", prep_f32[:, 1024 * (2 + rb):1024 * (3 + rb)],
            w2[128 * fb:128 * (fb + 1), :],
            [], [EB(4 + 2 * rb), EB(5 + 2 * rb)],
            tag=f"w2ld{fb}", q=("wld", 2 + rb))
        sch.op("sc", lambda e, rb=rb: e.activation(
            out=wring[:, 2 + rb, :], in_=prep_f32[:, 1024 * (2 + rb):1024 * (3 + rb)],
            func=AF.Identity),
            [EB(4 + 2 * rb), EB(5 + 2 * rb)], [B_("wring", 2 + rb)],
            tag=f"w2c{fb}")
        dma("sc", st_w2[fb], wring[:, 2 + rb, :], [B_("wring", 2 + rb)],
            [B_("st_w2", fb)], tag=f"w2st{fb}", q=("stg2", rb))

    for fb in range(NFB):
        thunks_w12.append(lambda fb=fb: _w1_step(fb))
        thunks_w12.append(lambda fb=fb: _w2_step(fb))

    GRID = 32768.0
    RBIG = 8388608.0
    _CONSTS = (GRID, 1.0 / GRID, RBIG, -256.0, 2.0 * D, 2.0 * DFF)
    for _i, _v in enumerate(_CONSTS):
        nc.const_aps.register(F32, _v, constcol[:, _i:_i + 1])
        sch.op("gp", lambda e, _i=_i, _v=_v: e.memset(constcol[:, _i:_i + 1], _v),
               [], [B_("constcol")], tag=f"const{_i}")
    Q1 = prep_f32[:, 1024:2048]
    Q2 = prep_f32[:, 2048:3072]
    Q3 = prep_f32[:, 3072:4096]

    def emit_split_sum(wq, wbufs, shi, slo, first, statbuf, base, tag):
        A, HQ, LQ = E8(base + 1), E8(base + 2), E8(base + 3)
        ab, hb, lb = EB(base + 1), EB(base + 2), EB(base + 3)
        sch.op("sc", lambda e: e.activation(out=A, in_=wq, func=AF.Abs),
               wbufs, [ab], tag=tag + "a")
        sch.op("sc", lambda e: e.activation(out=HQ, in_=A, func=AF.Identity,
                                            scale=GRID, bias=RBIG),
               [ab, B_("constcol")], [hb], tag=tag + "h1")
        sch.op("sc", lambda e: e.activation(out=HQ, in_=HQ, func=AF.Identity,
                                            scale=1.0 / GRID, bias=-256.0),
               [hb, B_("constcol")], [hb], tag=tag + "h2")
        sch.op("gp", lambda e: e.tensor_sub(LQ, A, HQ),
               [ab, hb], [lb], tag=tag + "lo")
        if first:
            def _r(e):
                e.reduce_sum(shi, HQ, axis=mybir.AxisListType.X)
                e.drain()
                return e.reduce_sum(slo, LQ, axis=mybir.AxisListType.X)
            sch.op("dve", _r, [hb, lb], [statbuf], tag=tag + "r")
        else:
            sx = 6 if base == 0 else 7

            def _r(e, sx=sx):
                e.reduce_sum(prep_vec[:, sx:sx + 1], HQ,
                             axis=mybir.AxisListType.X)
                e.drain()
                e.tensor_scalar(shi, shi, prep_vec[:, sx:sx + 1], 0.0,
                                ALU.add, ALU.add)
                e.drain()
                e.reduce_sum(prep_vec[:, sx:sx + 1], LQ,
                             axis=mybir.AxisListType.X)
                e.drain()
                return e.tensor_scalar(slo, slo, prep_vec[:, sx:sx + 1], 0.0,
                                       ALU.add, ALU.add)
            sch.op("dve", _r, [hb, lb, statbuf],
                   [statbuf, B_("prep_vec")], tag=tag + "r")

    def emit_tern(wq, wbufs, shi, negshi, slo, scale, dst, dstbufs, statbuf,
                  base, tag, n):
        S1, S2, OT = E8(base + 1), E8(base + 2), E8(base + 3)
        s1b, s2b, ob = EB(base + 1), EB(base + 2), EB(base + 3)
        sch.op("sc", lambda e: e.activation(out=S1, in_=wq, func=AF.Identity,
                                            scale=2.0 * n, bias=negshi),
               wbufs + [statbuf, B_("constcol")], [s1b], tag=tag + "s1")
        sch.op("sc", lambda e: e.activation(out=S2, in_=wq, func=AF.Identity,
                                            scale=2.0 * n, bias=shi),
               wbufs + [statbuf, B_("constcol")], [s2b], tag=tag + "s2")
        sch.op("dve", lambda e: e.tensor_scalar(S1, S1, slo, 0.0,
                                                ALU.subtract, ALU.is_gt),
               [s1b, statbuf], [s1b], tag=tag + "g")
        sch.op("dve", lambda e: e.tensor_scalar(S2, S2, slo, 0.0,
                                                ALU.add, ALU.is_lt),
               [s2b, statbuf], [s2b], tag=tag + "l")

        def _wt(e):
            e.tensor_sub(OT, S1, S2)
            e.drain()
            return e.tensor_scalar(dst, OT, scale, 0.0, ALU.mult, ALU.add)
        sch.op("dve", _wt, [s1b, s2b, statbuf], [ob] + dstbufs, tag=tag + "w")

    thunks_fc = []

    def _fc_step(ft):
        for hf in range(2):
            base = 4 * hf
            dma("sp", E8(base), wfc[128 * ft:128 * (ft + 1),
                                    512 * hf:512 * (hf + 1)],
                [], [EB(base)], tag=f"fc_ld{ft}_{hf}", q=("prep", hf))
            emit_split_sum(E8(base), [EB(base)], prep_vec[:, 0:1],
                           prep_vec[:, 1:2], hf == 0, B_("prep_vec"),
                           base, f"fc{ft}_{hf}")

        def _sc2(e):
            e.tensor_scalar(prep_vec[:, 3:4], prep_vec[:, 0:1],
                            prep_vec[:, 1:2], 0.0, ALU.add, ALU.add)
            e.tensor_scalar(prep_vec[:, 4:5], prep_vec[:, 0:1], -1.0, 0.0,
                            ALU.mult, ALU.add)
            e.drain()
            return e.tensor_scalar(prep_vec[:, 3:4], prep_vec[:, 3:4],
                                   1.0 / D, 1e-5, ALU.mult, ALU.max)

        sch.op("dve", _sc2, [B_("prep_vec")], [B_("prep_vec")],
               tag=f"fc{ft}_sc")
        for hf in range(2):
            base = 4 * hf
            emit_tern(E8(base), [EB(base)], prep_vec[:, 0:1],
                      prep_vec[:, 4:5], prep_vec[:, 1:2], prep_vec[:, 3:4],
                      prep_a[:, 512 * hf:512 * (hf + 1)],
                      [B_("prep_a", hf)], B_("prep_vec"), base,
                      f"fc{ft}t{hf}", D)
        for dt in range(NDT):
            sch.op("pe", lambda e, dt=dt: e.transpose(
                bank_bf(5, 128 * dt, 128 * (dt + 1)),
                prep_a[:, 128 * dt:128 * (dt + 1)], ident_bf[:, :]),
                [B_("prep_a", dt // 4), B_("ident_bf")], [PB(5)],
                tag=f"fcT{ft}_{dt}")
        sch.op("dve", lambda e: e.tensor_copy(
            out=prep_st[:, :], in_=bank_bf(5, 0, 1024)),
            [PB(5)], [B_("prep_st", d2) for d2 in range(NDT)],
            tag=f"fcTe{ft}")
        dma("sc", st_wfcT[ft], prep_st[:, :],
            [B_("prep_st", d2) for d2 in range(NDT)],
            [B_("st_wfcT", ft)], tag=f"fc_st{ft}", q=("stgfc",))

    for ft in range(NFB):
        thunks_fc.append(lambda ft=ft: _fc_step(ft))

    thunks_pr1 = []

    def _pr1_step(dt):
        for qq in range(4):
            for hf in range(2):
                base = 4 * hf
                c0 = 1024 * qq + 512 * hf
                dma("sp", E8(base), wpr[128 * dt:128 * (dt + 1), c0:c0 + 512],
                    [], [EB(base)], tag=f"pr_ld{dt}_{qq}_{hf}",
                    q=("prep", hf))
                emit_split_sum(E8(base), [EB(base)],
                               prscale[:, NDT + dt:NDT + dt + 1],
                               prscale[:, 2 * NDT + dt:2 * NDT + dt + 1],
                               qq == 0 and hf == 0, B_("prscale"),
                               base, f"pr{dt}_{qq}_{hf}")

        def _psc(e, dt=dt):
            e.tensor_scalar(prscale[:, dt:dt + 1],
                            prscale[:, NDT + dt:NDT + dt + 1],
                            prscale[:, 2 * NDT + dt:2 * NDT + dt + 1],
                            0.0, ALU.add, ALU.add)
            e.tensor_scalar(prscale[:, 3 * NDT + dt:3 * NDT + dt + 1],
                            prscale[:, NDT + dt:NDT + dt + 1], -1.0, 0.0,
                            ALU.mult, ALU.add)
            e.drain()
            return e.tensor_scalar(prscale[:, dt:dt + 1], prscale[:, dt:dt + 1],
                                   1.0 / DFF, 1e-5, ALU.mult, ALU.max)

        sch.op("dve", _psc, [B_("prscale")], [B_("prscale")],
               tag=f"pr_sc{dt}")

    for dt in range(NDT):
        thunks_pr1.append(lambda dt=dt: _pr1_step(dt))

    thunks_pr2 = []

    def _pr2_i(dt, i8):
        par = i8 % 2
        base = 4 * par
        c0 = 512 * i8
        dma("sp", E8(base), wpr[128 * dt:128 * (dt + 1), c0:c0 + 512],
            [], [EB(base)], tag=f"prf_ld{dt}_{i8}", q=("prep", par))
        emit_tern(E8(base), [EB(base)],
                  prscale[:, NDT + dt:NDT + dt + 1],
                  prscale[:, 3 * NDT + dt:3 * NDT + dt + 1],
                  prscale[:, 2 * NDT + dt:2 * NDT + dt + 1],
                  prscale[:, dt:dt + 1],
                  prep_a[:, 512 * par:512 * (par + 1)],
                  [B_("prep_a", par)], B_("prscale"), base,
                  f"prf{dt}_{i8}", DFF)
        for k in range(4):
            sch.op("pe", lambda e, k=k, par=par: e.transpose(
                bank_bf(5, 512 * par + 128 * k, 512 * par + 128 * (k + 1)),
                prep_a[:, 512 * par + 128 * k:512 * par + 128 * (k + 1)],
                ident_bf[:, :]),
                [B_("prep_a", par), B_("ident_bf")], [PB(5)],
                tag=f"prfT{dt}_{i8}_{k}")
        sch.op("dve", lambda e, par=par: e.tensor_copy(
            out=prep_st[:, 512 * par:512 * (par + 1)],
            in_=bank_bf(5, 512 * par, 512 * (par + 1))),
            [PB(5)], [B_("prep_st", par)], tag=f"prfTe{dt}_{i8}")
        for k in range(4):
            fc = 4 * i8 + k
            dma("sc", st_wprT[fc][:, 128 * dt:128 * (dt + 1)],
                prep_st[:, 512 * par + 128 * k:512 * par + 128 * (k + 1)],
                [B_("prep_st", par)],
                [B_("st_wprT", fc, dt)], tag=f"prst{dt}_{fc}",
                q=("stgpr", par))

    for i8 in range(8):
        for dt in range(NDT):
            thunks_pr2.append(lambda dt=dt, i8=i8: _pr2_i(dt, i8))

    AXX = mybir.AxisListType.X

    for c in range(NCHUNK):
        w0 = c * TC
        for tt in range(NTT):
            xb = tt % 2
            dma("sp", xbuf[:, xb, :], xh[w0 + 128 * tt: w0 + 128 * (tt + 1), :],
                [], [B_("xbuf", xb)], tag=f"c{c}xld{tt}", q=("x", xb))
            xt = xbuf[:, xb, :]
            sp_ = 8 * tt

            def _stats(e, xt=xt, sp_=sp_):
                e.reduce_sum(stat[:, sp_ + 0:sp_ + 1], xt, axis=AXX)
                e.scalar_tensor_tensor(out=sqscr[:, :], in0=xt, scalar=1.0,
                                       in1=xt, op0=ALU.mult, op1=ALU.mult,
                                       accum_out=stat[:, sp_ + 1:sp_ + 2])
                e.drain()
                e.tensor_scalar(stat[:, sp_ + 2:sp_ + 3], stat[:, sp_ + 0:sp_ + 1],
                                1.0 / D, None, ALU.mult)
                e.drain()
                e.tensor_scalar(stat[:, sp_ + 3:sp_ + 4], stat[:, sp_ + 2:sp_ + 3],
                                stat[:, sp_ + 2:sp_ + 3], EPS,
                                ALU.mult, ALU.subtract)
                e.drain()
                return e.scalar_tensor_tensor(
                    out=stat[:, sp_ + 4:sp_ + 5], in0=stat[:, sp_ + 1:sp_ + 2],
                    scalar=1.0 / D, in1=stat[:, sp_ + 3:sp_ + 4],
                    op0=ALU.mult, op1=ALU.subtract)

            sch.op("dve", _stats, [B_("xbuf", xb)],
                   [B_("stat", tt), B_("sqh", 0), B_("sqh", 1)],
                   tag=f"c{c}stats{tt}")
            sch.op("sc", lambda e, sp_=sp_: e.activation(
                out=stat[:, sp_ + 5:sp_ + 6], in_=stat[:, sp_ + 4:sp_ + 5],
                func=AF.Sqrt),
                [B_("stat", tt)], [B_("stat", tt)], tag=f"c{c}sqrt{tt}")

            def _rstd(e, sp_=sp_):
                e.reciprocal(stat[:, sp_ + 6:sp_ + 7], stat[:, sp_ + 5:sp_ + 6])
                e.drain()
                return e.tensor_scalar(stat[:, sp_ + 7:sp_ + 8],
                                       stat[:, sp_ + 2:sp_ + 3],
                                       stat[:, sp_ + 6:sp_ + 7],
                                       -1.0, ALU.mult, ALU.mult)

            sch.op("dve", _rstd, [B_("stat", tt)], [B_("stat", tt)],
                   tag=f"c{c}rstd{tt}")
            sch.op("sc", lambda e, xt=xt, tt=tt, sp_=sp_: e.activation(
                out=vp[:, tt, :, 0:64],
                in_=xt.rearrange("p (h d) -> p h d", h=H),
                func=AF.Identity, bias=stat[:, sp_ + 7:sp_ + 8],
                scale=stat[:, sp_ + 6:sp_ + 7]),
                [B_("xbuf", xb), B_("stat", tt)], [B_("vp", tt)],
                tag=f"c{c}vp{tt}")
        if c == 0:
            for t_ in thunks_w12:
                t_()
        for tt in range(NTT):
            for g in range(2):
                pb = 4 + g
                for k in range(4):
                    dt = g * 4 + k
                    for hh in (2 * dt, 2 * dt + 1):
                        p0 = (hh % 2) * 64
                        sch.op("pe", lambda e, tt=tt, pb=pb, k=k, hh=hh, p0=p0:
                               e.transpose(
                                   psum_bf[p0:p0 + 64,
                                           1024 * pb + 128 * k:1024 * pb + 128 * (k + 1)],
                                   vp[:, tt, hh, 0:64], ident_bf[:, :]),
                               [B_("vp", tt), B_("ident_bf")], [PB(pb)],
                               tag=f"c{c}xT{tt}_{hh}")
                sch.op("dve", lambda e, tt=tt, g=g, pb=pb: e.tensor_copy(
                    out=xnT[:, 4 * g:4 * (g + 1), 128 * tt:128 * (tt + 1)],
                    in_=bank_bf(pb, 0, 512).rearrange("p (a b) -> p a b", a=4)),
                    [PB(pb)], [B_("xnT", tt, g)], tag=f"c{c}xTe{tt}_{g}")
            if 2 <= tt < 2 + NQT:
                qi = tt - 2
                for dt in range(NDT):
                    sch.op("sc", lambda e, dt=dt, tt=tt:
                           e.activation(out=xnT2[:, dt % 2, :],
                                        in_=xnT[:, dt, 128 * tt:128 * (tt + 1)],
                                        func=AF.Square),
                           [B_("xnT", tt, dt // 4)], [B_("xnT2", dt % 2)],
                           tag=f"c{c}sq{tt}_{dt}")
                    sch.op("pe", lambda e, dt=dt, qi=qi: e.matmul(
                        bank(6, 128 * qi, 128 * (qi + 1), part=16),
                        headmask[:, dt, :], xnT2[:, dt % 2, :],
                        start=(dt == 0), stop=(dt == NDT - 1)),
                        [B_("hm", dt), B_("xnT2", dt % 2)], [PB(6)],
                        tag=f"c{c}mm{tt}_{dt}")
                sch.op("dve", lambda e, qi=qi: e.tensor_scalar_mul(
                    m16[:, 128 * qi:128 * (qi + 1)],
                    bank(6, 128 * qi, 128 * (qi + 1), part=16), -1.0),
                    [PB(6)], [B_("m16", qi)], tag=f"c{c}me{tt}")

        NKB = WINT // 128
        KB_ORDER = [3] + [k for k in range(NKB) if k != 3]

        def emit_aug(h):
            hs = h % 3
            dt_h, r0 = h // 2, (h % 2) * 64
            xnT_keys = [B_("xnT", tt2, dt_h // 4) for tt2 in range(NTT)]
            dma("sc", kaug[0:64, hs, :], xnT[r0:r0 + 64, dt_h, :],
                xnT_keys, [B_("kaug", hs)], tag=f"c{c}h{h}kaug",
                q=("kaug", hs))
            dma("sc", qaug[0:64, hs, :], xnT[r0:r0 + 64, dt_h, HALO:HALO + TC],
                xnT_keys, [B_("qaug", hs)], tag=f"c{c}h{h}qaug",
                q=("qaug", hs))
            dma("sc", qaug[64:65, hs, :], m16[h:h + 1, :],
                [B_("m16", qi2) for qi2 in range(NQT)], [B_("qaugm", hs)],
                tag=f"c{c}h{h}qm", q=("qaugm", hs))

        emit_aug(0)
        emit_aug(1)
        for h in range(H):
            hs = h % 3
            avb = 6 + (h % 2)
            ptb = h % 2

            def emit_post(idx, h=h, hs=hs, avb=avb):
                ikb = KB_ORDER[idx]
                qlo = max(ikb * 128 - 256, HALO)
                qhi = min(ikb * 128 + 384, HALO + TC)
                ql, qh = qlo - HALO, qhi - HALO
                w = qh - ql
                sbk = 2 + (idx % 3)
                ps = idx % 3
                sch.op("sc", lambda e, sbk=sbk, w=w, ps=ps: e.activation(
                    out=psb[:, ps, 0:w], in_=bank(sbk, 0, w), func=AF.Exp),
                    [PB(sbk)], [B_("psb", ps)], tag=f"c{c}h{h}e{ikb}")
                j0 = qlo - (ikb * 128 - 256)
                for jr0, jr1 in ((0, 128), (513, 640)):
                    lo = max(jr0, j0)
                    hi = min(jr1, j0 + w)
                    if lo < hi:
                        sch.op("gp", lambda e, ps=ps, lo=lo, hi=hi, j0=j0:
                               e.tensor_mul(psb[:, ps, lo - j0:hi - j0],
                                            psb[:, ps, lo - j0:hi - j0],
                                            maskband[:, lo:hi]),
                               [B_("psb", ps), B_("maskband")], [B_("psb", ps)],
                               tag=f"c{c}h{h}m{ikb}_{lo}")
                sch.op("pe", lambda e, ikb=ikb, h=h, ps=ps, ql=ql, qh=qh,
                       avb=avb, w=w: e.matmul(
                           bank(avb, ql, qh, part=65),
                           vp[:, ikb, h, 0:65], psb[:, ps, 0:w],
                           start=(ikb == KB_ORDER[0]),
                           stop=(ikb == KB_ORDER[-1])),
                       [B_("psb", ps), B_("vp", ikb), B_("vp_ones", ikb)],
                       [PB(avb)], tag=f"c{c}h{h}av{ikb}")

            def emit_smm(idx, h=h, hs=hs):
                ikb = KB_ORDER[idx]
                qlo = max(ikb * 128 - 256, HALO)
                qhi = min(ikb * 128 + 384, HALO + TC)
                ql, qh = qlo - HALO, qhi - HALO
                sbk = 2 + (idx % 3)
                sch.op("pe", lambda e, hs=hs, ikb=ikb, ql=ql, qh=qh, sbk=sbk:
                       e.matmul(bank(sbk, 0, qh - ql),
                                kaug[0:65, hs, 128 * ikb:128 * (ikb + 1)],
                                qaug[0:65, hs, ql:qh], start=True, stop=True),
                       [B_("kaug", hs), B_("kaug_ones", hs), B_("qaug", hs),
                        B_("qaugm", hs)],
                       [PB(sbk)], tag=f"c{c}h{h}s{ikb}")

            emit_smm(0)
            emit_smm(1)
            if h + 2 < H:
                emit_aug(h + 2)
            for idx in range(NKB):
                if idx + 2 < NKB:
                    emit_smm(idx + 2)
                emit_post(idx)
            sch.op("dve", lambda e, a2=h % 2, avb=avb: e.tensor_copy(
                out=avT[0:65, a2, :], in_=bank(avb, 0, 512, part=65)),
                [PB(avb)], [B_("avT", h % 2)], tag=f"c{c}h{h}avE")
            for tb in range(NQT):
                sch.op("pe", lambda e, a2=h % 2, tb=tb, ptb=ptb: e.transpose(
                    bank(ptb, 65 * tb, 65 * (tb + 1)),
                    avT[0:65, a2, 128 * tb:128 * (tb + 1)],
                    ident_f32[0:65, 0:65]),
                    [B_("avT", h % 2), B_("ident_f32")], [PB(ptb)],
                    tag=f"c{c}h{h}T{tb}")
            sch.op("dve", lambda e, ptb=ptb: e.reciprocal(
                out=rz[:, 0:NQT],
                in_=bank(ptb, 0, 65 * NQT).rearrange(
                    "p (s x) -> p s x", x=65)[:, :, 64]),
                [PB(ptb)], [B_("rz")], tag=f"c{c}h{h}rz")
            for tb in range(NQT):
                sch.op("dve", lambda e, h=h, tb=tb, ptb=ptb:
                       e.scalar_tensor_tensor(
                           out=ln1in[:, tb, 64 * h:64 * (h + 1)],
                           in0=bank(ptb, 65 * tb, 65 * tb + 64),
                           scalar=rz[:, tb:tb + 1],
                           in1=vp[:, 2 + tb, h, 0:64],
                           op0=ALU.mult, op1=ALU.add),
                       [PB(ptb), B_("rz"), B_("vp", 2 + tb),
                        B_("vp_ones", 2 + tb)],
                       [B_("ln1", tb, h)], tag=f"c{c}h{h}j{tb}")

        if c == 0:
            for t_ in thunks_fc:
                t_()
        if "ln1in" in dbg:
            for tb in range(NQT):
                dma("sc", dbg["ln1in"][c, tb], ln1in[:, tb, :],
                    [B_("ln1", tb, h2) for h2 in range(H)],
                    [B_("dbgl1", c, tb)], tag=f"dbgl1{c}_{tb}",
                    q=("dbg", c), barrier=True)

        def ln_stats(x_ap, reads, par, tag):
            sp_ = 8 * par

            def _st(e, x_ap=x_ap, sp_=sp_):
                e.reduce_sum(stat[:, sp_ + 0:sp_ + 1], x_ap, axis=AXX)
                e.scalar_tensor_tensor(out=sqscr[:, :], in0=x_ap, scalar=1.0,
                                       in1=x_ap, op0=ALU.mult, op1=ALU.mult,
                                       accum_out=stat[:, sp_ + 1:sp_ + 2])
                e.drain()
                e.tensor_scalar(stat[:, sp_ + 2:sp_ + 3],
                                stat[:, sp_ + 0:sp_ + 1], 1.0 / D, None,
                                ALU.mult)
                e.drain()
                e.tensor_scalar(stat[:, sp_ + 3:sp_ + 4],
                                stat[:, sp_ + 2:sp_ + 3],
                                stat[:, sp_ + 2:sp_ + 3], EPS,
                                ALU.mult, ALU.subtract)
                e.drain()
                return e.scalar_tensor_tensor(
                    out=stat[:, sp_ + 4:sp_ + 5], in0=stat[:, sp_ + 1:sp_ + 2],
                    scalar=1.0 / D, in1=stat[:, sp_ + 3:sp_ + 4],
                    op0=ALU.mult, op1=ALU.subtract)

            sch.op("dve", _st, reads, [B_("stat", par), B_("sqh", 0), B_("sqh", 1)],
                   tag=tag + "_st")
            sch.op("sc", lambda e, sp_=sp_: e.activation(
                out=stat[:, sp_ + 5:sp_ + 6], in_=stat[:, sp_ + 4:sp_ + 5],
                func=AF.Sqrt),
                [B_("stat", par)], [B_("stat", par)], tag=tag + "_sq")
            sch.op("dve", lambda e, sp_=sp_: e.reciprocal(
                stat[:, sp_ + 6:sp_ + 7], stat[:, sp_ + 5:sp_ + 6]),
                [B_("stat", par)], [B_("stat", par)], tag=tag + "_r")
            return stat[:, sp_ + 2:sp_ + 3], stat[:, sp_ + 6:sp_ + 7]

        for tb in range(NQT):
            l1keys = [B_("ln1", tb, h2) for h2 in range(H)]
            mu_ap, rstd_ap = ln_stats(ln1in[:, tb, :], l1keys, tb,
                                      f"c{c}l1s{tb}")
            sch.op("dve", lambda e, tb=tb, mu_ap=mu_ap, rstd_ap=rstd_ap:
                   e.tensor_scalar(t1s[:, tb, :], ln1in[:, tb, :],
                                   mu_ap, rstd_ap, ALU.subtract, ALU.mult),
                   l1keys + [B_("stat", tb)], [B_("t1s", tb)],
                   tag=f"c{c}t1_{tb}")
        for tb in range(NQT):
            sch.op("gp", lambda e, tb=tb: e.tensor_mul(
                x1tok[:, tb, :], t1s[:, tb, :], g1bc[:, :]),
                [B_("t1s", tb), B_("g1bc")], [B_("x1tok", tb)],
                tag=f"c{c}xg{tb}")
        for dt in range(NDT):
            pb = 4 + dt % 2
            for tb in range(NQT):
                sch.op("pe", lambda e, tb=tb, dt=dt, pb=pb: e.transpose(
                    bank_bf(pb, 128 * tb, 128 * (tb + 1)),
                    t1s[:, tb, 128 * dt:128 * (dt + 1)], ident_bf[:, :]),
                    [B_("t1s", tb), B_("ident_bf")], [PB(pb)],
                    tag=f"c{c}x1T{tb}_{dt}")
            sch.op("dve", lambda e, dt=dt, pb=pb: e.tensor_scalar(
                x1T[:, dt, :], bank_bf(pb, 0, 512),
                g1b1_sb[:, dt:dt + 1],
                g1b1_sb[:, NDT + dt:NDT + dt + 1],
                ALU.mult, ALU.add),
                [PB(pb), B_("g1"), B_("b1ln")],
                [B_("x1T", tb2, dt) for tb2 in range(NQT)],
                tag=f"c{c}x1Te{dt}")

        if c == 0:
            for t_ in thunks_pr1:
                t_()
        for fb in range(NFB):
            rs = fb % 4
            dma("sp", wring[:, rs, :], st_w1[fb],
                [B_("st_w1", fb)], [B_("wring", rs)],
                tag=f"c{c}w1l{fb}", q=("ring", rs))
            hb = 4 + (fb % 2)
            for dt in range(NDT):
                sch.op("pe", lambda e, rs=rs, dt=dt, hb=hb: e.matmul(
                    bank(hb), wring[:, rs, 128 * dt:128 * (dt + 1)],
                    x1T[:, dt, :], start=(dt == 0), stop=(dt == NDT - 1)),
                    [B_("wring", rs)] + [B_("x1T", tb2, dt)
                                         for tb2 in range(NQT)],
                    [PB(hb)], tag=f"c{c}ff1_{fb}_{dt}")
            sch.op("sc", lambda e, fb=fb, hb=hb: e.activation(
                out=pool[:, fb, :], in_=bank(hb), func=AF.Relu,
                bias=bias1_sb[:, fb:fb + 1]),
                [PB(hb), B_("bias1")], [B_("pool", fb)],
                tag=f"c{c}relu{fb}")

        for tb in range(NQT):
            for dg in range(2):
                bi = 2 * tb + dg
                sch.op("pe", lambda e, bi=bi, dg=dg: e.matmul(
                    bank(bi), ones1pf[0:1, :],
                    b2row[0:1, 512 * dg:512 * (dg + 1)],
                    start=True, stop=False),
                    [B_("ones1pf"), B_("b2row")], [PB(bi)],
                    tag=f"c{c}b2_{tb}_{dg}")
                sch.op("pe", lambda e, bi=bi, dg=dg: e.matmul(
                    bank(bi), ones1pf[0:1, :],
                    b1rowT[0:1, 512 * dg:512 * (dg + 1)],
                    start=False, stop=False),
                    [B_("ones1pf"), B_("b1row")], [PB(bi)],
                    tag=f"c{c}bb1_{tb}_{dg}")
        for fb in range(NFB):
            rs = fb % 4
            dma("sp", wring[:, rs, :], st_w2[fb],
                [B_("st_w2", fb)], [B_("wring", rs)],
                tag=f"c{c}w2l{fb}", q=("ring", rs))
            for tb in range(NQT):
                for dg in range(2):
                    bi = 2 * tb + dg
                    sch.op("pe", lambda e, rs=rs, fb=fb, tb=tb, dg=dg, bi=bi:
                           e.matmul(bank(bi),
                                    pool[:, fb, 128 * tb:128 * (tb + 1)],
                                    wring[:, rs, 512 * dg:512 * (dg + 1)],
                                    start=False, stop=(fb == NFB - 1)),
                           [B_("pool", fb), B_("wring", rs)], [PB(bi)],
                           tag=f"c{c}ff2_{fb}_{tb}_{dg}")

        for tb in range(NQT):
            l1keys = [B_("ln1", tb, h2) for h2 in range(H)]
            sch.op("dve", lambda e, tb=tb: e.scalar_tensor_tensor(
                out=ln1in[:, tb, :], in0=bank(2 * tb, 0, 512 * 2),
                scalar=1.0, in1=x1tok[:, tb, :],
                op0=ALU.bypass, op1=ALU.add),
                [PB(2 * tb), PB(2 * tb + 1), B_("x1tok", tb)],
                l1keys, tag=f"c{c}l2j{tb}")
        for dt in range(NDT):
            sch.op("pe", lambda e, dt=dt: e.matmul(
                bank(4, 0, 512, part=16), vT_sb[:, 16 * dt:16 * (dt + 1)],
                xnT[:, dt, HALO:HALO + TC],
                start=(dt == 0), stop=(dt == NDT - 1)),
                [B_("vT", dt)] + [B_("xnT", tt2, dt // 4)
                                  for tt2 in range(NTT)],
                [PB(4)], tag=f"c{c}t1T_{dt}")
        sch.op("dve", lambda e: e.tensor_copy(out=lr_sb[0:16, 0, :],
                                              in_=bank(4, 0, 512, part=16)),
               [PB(4)], [B_("lr", 0)], tag=f"c{c}t1Te")
        sch.op("pe", lambda e: e.matmul(bank(5, 0, 512, part=16),
                                        fbT_sb[:, :], lr_sb[0:16, 0, :],
                                        start=True, stop=True),
               [B_("fbT"), B_("lr", 0)], [PB(5)], tag=f"c{c}t2T")
        sch.op("dve", lambda e: e.tensor_copy(out=lr_sb[0:16, 1, :],
                                              in_=bank(5, 0, 512, part=16)),
               [PB(5)], [B_("lr", 1)], tag=f"c{c}t2Te")
        for fc in range(NFB):
            rs = fc % 4
            dma("sp", wring[:, rs, :], st_wfcT[fc],
                [B_("st_wfcT", fc)], [B_("wring", rs)],
                tag=f"c{c}wfl{fc}", q=("ring", rs))
            hb = 4 + (fc % 2)
            for dt in range(NDT):
                sch.op("pe", lambda e, rs=rs, dt=dt, hb=hb: e.matmul(
                    bank(hb), wring[:, rs, 128 * dt:128 * (dt + 1)],
                    xnT[:, dt, HALO:HALO + TC],
                    start=(dt == 0), stop=False),
                    [B_("wring", rs)] + [B_("xnT", tt2, dt // 4)
                                         for tt2 in range(NTT)],
                    [PB(hb)], tag=f"c{c}fc{fc}_{dt}")
            sch.op("pe", lambda e, fc=fc, hb=hb: e.matmul(
                bank(hb), uTg_sb[0:16, 128 * fc:128 * (fc + 1)],
                lr_sb[0:16, 1, :], start=False, stop=True),
                [B_("uTg"), B_("lr", 1)], [PB(hb)], tag=f"c{c}u{fc}")
            sh = fc % 2
            sch.op("sc", lambda e, hb=hb, sh=sh: e.activation(
                out=sqscr[:, 512 * sh:512 * (sh + 1)], in_=bank(hb),
                func=AF.Relu),
                [PB(hb)], [B_("sqh", sh)], tag=f"c{c}rl2_{fc}")
            sch.op("dve", lambda e, fc=fc, hb=hb, sh=sh: e.tensor_mul(
                pool[:, fc, :], sqscr[:, 512 * sh:512 * (sh + 1)], bank(hb)),
                [PB(hb), B_("sqh", sh)], [B_("pool", fc)], tag=f"c{c}sq2_{fc}")

        if c == 0:
            for t_ in thunks_pr2:
                t_()
        for tb in range(NQT):
            l1keys = [B_("ln1", tb, h2) for h2 in range(H)]
            mu2, rstd2 = ln_stats(ln1in[:, tb, :], l1keys, 4 + tb,
                                  f"c{c}l2s{tb}")
            sch.op("dve", lambda e, tb=tb, mu2=mu2, rstd2=rstd2:
                   e.tensor_scalar(sqscr[:, :], ln1in[:, tb, :],
                                   mu2, rstd2, ALU.subtract, ALU.mult),
                   l1keys + [B_("stat", 4 + tb)], [B_("sqh", 0), B_("sqh", 1)],
                   tag=f"c{c}z2_{tb}")
            sch.op("gp", lambda e, tb=tb: e.tensor_mul(
                t1s[:, tb, :], sqscr[:, :], g2bc[:, :]),
                [B_("sqh", 0), B_("sqh", 1), B_("g2bc")], [B_("t1s", tb)],
                tag=f"c{c}g2z{tb}")

        for tb in range(NQT):
            for dg in range(2):
                bi = 2 * tb + dg
                sch.op("pe", lambda e, bi=bi, dg=dg: e.matmul(
                    bank(bi), ones1pf[0:1, :],
                    bt2row[0:1, 512 * dg:512 * (dg + 1)],
                    start=True, stop=False),
                    [B_("ones1pf"), B_("bt2row")], [PB(bi)],
                    tag=f"c{c}bt2_{tb}_{dg}")
        for fc in range(NFB):
            rs = fc % 4
            dma("sp", wring[:, rs, :], st_wprT[fc],
                [B_("st_wprT", fc, dt2) for dt2 in range(NDT)],
                [B_("wring", rs)], tag=f"c{c}wpl{fc}", q=("ring", rs))
            for tb in range(NQT):
                for dg in range(2):
                    bi = 2 * tb + dg
                    sch.op("pe", lambda e, rs=rs, fc=fc, tb=tb, dg=dg, bi=bi:
                           e.matmul(bank(bi),
                                    pool[:, fc, 128 * tb:128 * (tb + 1)],
                                    wring[:, rs, 512 * dg:512 * (dg + 1)],
                                    start=False, stop=(fc == NFB - 1)),
                           [B_("pool", fc), B_("wring", rs)], [PB(bi)],
                           tag=f"c{c}pj{fc}_{tb}_{dg}")

        for tb in range(NQT):
            xb2 = tb % 2
            dma("sp", xbuf[:, xb2, :],
                xh[HALO + c * TC + 128 * tb: HALO + c * TC + 128 * (tb + 1), :],
                [], [B_("xbuf", xb2)], tag=f"c{c}xr{tb}", q=("x", xb2))
            l1keys = [B_("ln1", tb, h2) for h2 in range(H)]
            sch.op("dve", lambda e, tb=tb, xb2=xb2: e.scalar_tensor_tensor(
                out=ln1in[:, tb, :], in0=bank(2 * tb, 0, 1024),
                scalar=1.0, in1=xbuf[:, xb2, :],
                op0=ALU.bypass, op1=ALU.add),
                [PB(2 * tb), PB(2 * tb + 1), B_("xbuf", xb2)],
                l1keys, tag=f"c{c}a1_{tb}")
            ob = tb % 2
            sch.op("dve", lambda e, tb=tb, ob=ob: e.tensor_add(
                outbuf[:, ob, :], ln1in[:, tb, :], t1s[:, tb, :]),
                l1keys + [B_("t1s", tb)], [B_("outb", ob)],
                tag=f"c{c}out{tb}")
            dma("sc", y[c * TC + 128 * tb: c * TC + 128 * (tb + 1), :],
                outbuf[:, ob, :], [B_("outb", ob)], [B_("y", c, tb)],
                tag=f"c{c}yst{tb}", q=("out", ob))

        if "vp" in dbg:
            for tt in range(NTT):
                dma("sc", dbg["vp"][c, tt], vp[:, tt, :, :], [B_("vp", tt), B_("vp_ones", tt)],
                    [B_("dbgvp", c, tt)], tag=f"dbgvp{c}_{tt}", q=("dbg", c), barrier=True)
        if "xnT" in dbg:
            for dt in range(NDT):
                dma("sc", dbg["xnT"][c, dt], xnT[:, dt, :],
                    [B_("xnT", tt2, dt // 4) for tt2 in range(NTT)],
                    [B_("dbgxnT", c, dt)], tag=f"dbgxnT{c}_{dt}", q=("dbg", c), barrier=True)
        if "m" in dbg:
            dma("sc", dbg["m"][c], m16[:, :], [B_("m16", qi) for qi in range(NQT)],
                [B_("dbgm", c)], tag=f"dbgm{c}", q=("dbg", c), barrier=True)

    import contextlib as _ctx
    with _ctx.ExitStack() as semstack:
        sems = {e: (semstack.enter_context(nc.semaphore(f"s_{e}"))
                    if e != "sp" else None) for e in ENGS}
        qkeys = sch.qkeys()
        qsems = {k: semstack.enter_context(
            nc.semaphore("q_" + "_".join(str(p) for p in k)))
            for k in qkeys}
        run = sch.emit(nc, sems, qsems)
        qeng = {}
        for o in sch.ops:
            if o.dma:
                qeng[o.q] = o.eng
        qfin = {e: [(qsems[k], sch.final_q[k]) for k in qkeys
                    if qeng[k] == e] for e in ENGS}
        with nc.Block() as blk:

            @blk.tensor
            def _(e):
                run("pe", e)

            @blk.vector
            def _(e):
                run("dve", e)

            @blk.scalar
            def _(e):
                run("sc", e)
                for s, v in qfin["sc"]:
                    e.wait_ge(s, v)

            @blk.sync
            def _(e):
                run("sp", e)
                for s, v in qfin["sp"]:
                    e.wait_ge(s, v)

            @blk.gpsimd
            def _(e):
                run("gp", e)
                for s, v in qfin["gp"]:
                    e.wait_ge(s, v)

    stack.close()
    return nc


_NC_CACHE = {}


def _get_nc():
    key = tuple(DEBUG_TAPS)
    if key not in _NC_CACHE:
        _NC_CACHE[key] = build_nc()
    return _NC_CACHE[key]


def kernel(L0_attn_src, L0_attn_ln1_g, L0_attn_ln1_b, L0_attn_ln2_g,
           L0_attn_ln2_b, L0_attn_W1, L0_attn_b1, L0_attn_W2, L0_attn_b2,
           L0_ffn_fast_b, L0_ffn_w_fc, L0_ffn_w_proj, L0_ffn_u, L0_ffn_v,
           L0_ffn_gate):
    x = np.asarray(L0_attn_src, dtype=np.float32)
    nc = _get_nc()
    common = {
        "w1": np.ascontiguousarray(L0_attn_W1, dtype=np.float32),
        "w2": np.ascontiguousarray(L0_attn_W2, dtype=np.float32),
        "wfc": np.ascontiguousarray(L0_ffn_w_fc, dtype=np.float32),
        "wpr": np.ascontiguousarray(L0_ffn_w_proj, dtype=np.float32),
        "g1": np.ascontiguousarray(L0_attn_ln1_g, dtype=np.float32),
        "b1ln": np.ascontiguousarray(L0_attn_ln1_b, dtype=np.float32),
        "g2": np.ascontiguousarray(L0_attn_ln2_g, dtype=np.float32),
        "b2ln": np.ascontiguousarray(L0_attn_ln2_b, dtype=np.float32),
        "bias1": np.ascontiguousarray(L0_attn_b1, dtype=np.float32),
        "bias2": np.ascontiguousarray(L0_attn_b2, dtype=np.float32),
        "fastb": np.ascontiguousarray(L0_ffn_fast_b, dtype=np.float32),
        "uu": np.ascontiguousarray(L0_ffn_u, dtype=np.float32),
        "vv": np.ascontiguousarray(L0_ffn_v, dtype=np.float32),
        "gate": np.asarray(L0_ffn_gate, dtype=np.float32).reshape(1, 1),
    }
    in_maps = []
    for core in range(NCORES):
        b, half = core // 2, core % 2
        lo = half * TCORE - HALO
        hi = half * TCORE + TCORE + HALO
        xhc = np.zeros((TCORE + 2 * HALO, D), dtype=np.float32)
        s0, s1 = max(lo, 0), min(hi, S)
        xhc[s0 - lo: s1 - lo] = x[b, s0:s1]
        m = dict(common)
        m["xh"] = xhc
        in_maps.append(m)
    res = run_bass_kernel_spmd(nc, in_maps, core_ids=list(range(NCORES)))
    out = np.empty((B, S, D), dtype=np.float32)
    for core in range(NCORES):
        b, half = core // 2, core % 2
        out[b, half * TCORE:(half + 1) * TCORE] = res.results[core]["y"]
    globals()["_LAST_RES"] = res
    return out
